# version 25 (final)
# speedup vs baseline: 1.3513x; 1.0036x over previous
import numpy as np
import ml_dtypes
from contextlib import ExitStack

import os
import concourse.bass as bass
import concourse.mybir as mybir
from concourse.bass_utils import run_bass_kernel_spmd

PARANOID = False
FREE = {"p1", "p2", "p3"}

F32 = mybir.dt.float32
BF16 = mybir.dt.bfloat16
I32 = mybir.dt.int32
AF = mybir.ActivationFunctionType
ALU = mybir.AluOpType
AX = mybir.AxisListType

H = 1024
KC = 8
S = 16
NE = 8
HE = 256
R = 64
EPS = 1e-5

ENGINES = ("sync", "gpsimd", "tensor", "vector", "scalar")


class Task:
    __slots__ = ("eng", "cnt")


class Sched:
    def __init__(self, nc, es):
        self.nc = nc
        self.sem = {e: es.enter_context(nc.semaphore(f"q_{e}")) for e in ENGINES}
        self.wbsem = {e: es.enter_context(nc.semaphore(f"wb_{e}"))
                      for e in ("vector", "scalar")}
        self.cnt = {e: 0 for e in ENGINES}
        self.items = {e: [] for e in ENGINES}
        self.bank_readers = {}
        self._last = []
        self.phase = "p0"

    def task(self, eng, emit, deps=(), dma=False):
        inc = 16 if dma else 1
        t = Task()
        t.eng = eng
        self.cnt[eng] += inc
        t.cnt = self.cnt[eng]
        deps = [d for d in deps if d is not None]
        if self.phase not in FREE:
            deps = deps + [p for p in self._last if p is not None]
        self._last = [t]
        self.items[eng].append((deps, emit, inc, t))
        return t

    def fence(self, eng=None):
        if eng is None:
            return [self.fence(e) for e in ENGINES]
        t = Task()
        t.eng = eng
        t.cnt = self.cnt[eng]
        return t

    def bank_deps(self, keys):
        deps = []
        for k in keys:
            deps.extend(self.bank_readers.get(k, ()))
        return deps

    def bank_read(self, keys, task):
        for k in keys:
            self.bank_readers.setdefault(k, []).append(task)

    def bank_wrote(self, keys):
        for k in keys:
            self.bank_readers[k] = []

    def emit(self, blk):
        clocks = {e: {e2: 0 for e2 in ENGINES} for e in ENGINES}
        wb = {e: self.wbsem[e] for e in ("vector", "scalar")}
        wbcnt = {e: 0 for e in ("vector", "scalar")}

        class _Proxy:
            def __init__(self, e):
                self._e = e
                self.insts = []

            def __getattr__(self, k):
                f = getattr(self._e, k)
                if not callable(f) or k in ("wait_ge", "nop"):
                    return f

                def g(*a, **kw):
                    r = f(*a, **kw)
                    if hasattr(r, "then_inc"):
                        self.insts.append(r)
                    return r
                return g

        def mk(name):
            def body(e):
                cl = clocks[name]
                fence = name in wb
                for deps, emitfn, inc, t in self.items[name]:
                    for d in deps:
                        if d.cnt > cl[d.eng]:
                            e.wait_ge(self.sem[d.eng], d.cnt)
                            cl[d.eng] = d.cnt
                    if fence:
                        px = _Proxy(e)
                        last = emitfn(px)
                        if len(px.insts) > 1:
                            for ins in px.insts:
                                ins.then_inc(wb[name], 1)
                                wbcnt[name] += 1
                            e.wait_ge(wb[name], wbcnt[name])
                            e.nop().then_inc(self.sem[name], inc)
                        else:
                            last.then_inc(self.sem[name], inc)
                    else:
                        emitfn(e).then_inc(self.sem[name], inc)
                if name == "gpsimd":
                    e.wait_ge(self.sem[name], self.cnt[name])
            return body

        blk.sync(mk("sync"))
        blk.gpsimd(mk("gpsimd"))
        blk.tensor(mk("tensor"))
        blk.vector(mk("vector"))
        blk.scalar(mk("scalar"))


def build_program(T=2048, NB=4):
    NT = NB * T // 128
    TPB = T // 128
    T3PB = T // 512
    NT3 = NB * T3PB
    assert T % 512 == 0 and TPB >= 2

    nc = bass.Bass("TRN2", target_bir_lowering=False,
                   detect_race_conditions=False, name=f"mkseq{T}x{NB}")

    def din(name, shape, dtype=F32):
        return nc.dram_tensor(name, list(shape), dtype, kind="ExternalInput").ap()

    d_x = din("x", (NB * T, H))
    d_win = din("win", (H, H)); d_wb = din("wb", (H, H))
    d_wd = din("wd", (H, H)); d_wout = din("wout", (H, H))
    d_wbase = din("wbase", (H, H))
    d_bmat = din("bmat", (H, S)); d_cmat = din("cmat", (S, H))
    d_wig = din("wig", (H, S)); d_wfg = din("wfg", (H, S))
    d_wenc = din("wenc", (NE, H, HE)); d_wdec = din("wdec", (NE, HE, H))
    d_wdn = din("wdn", (NE, H, R)); d_wup = din("wup", (NE, R, H))
    d_bin = din("bin", (1, H)); d_bb = din("bb", (1, H))
    d_bd = din("bd", (1, H)); d_bout = din("bout", (1, H))
    d_big = din("big", (1, S)); d_bfg = din("bfg", (1, S))
    d_gam = din("gam", (1, H)); d_bet = din("bet", (1, H))
    d_bbase = din("bbase", (1, H))
    d_benc = din("benc", (NE, HE)); d_bdec = din("bdec", (NE, H))
    d_bdn = din("bdn", (NE, R)); d_bup = din("bup", (NE, H))
    d_cidx = din("cidx", (1, NE), I32)
    d_idf = din("idf", (128, 128)); d_idb = din("idb", (128, 128), BF16)
    d_ones_r_bf = din("ones_r_bf", (1, 128), BF16)
    d_ones_r_f = din("ones_r_f", (1, 128))
    d_ones_cell_bf = din("ones_cell_bf", (1, 1), BF16)
    d_ones88 = din("ones88", (8, 8), BF16)

    d_out = nc.dram_tensor("out", [NB * T, H], F32, kind="ExternalOutput").ap()
    d_ln1t = nc.dram_tensor("sp_ln1t", [NT, 128, H], BF16, kind="Internal").ap()
    d_ln2t = nc.dram_tensor("sp_ln2t", [NT, 128, H], BF16, kind="Internal").ap()
    d_x1 = nc.dram_tensor("sp_x1", [NT, 128, H], F32, kind="Internal").ap()

    es = ExitStack()
    with es:
        E = es.enter_context
        sch = Sched(nc, es)
        TK = sch.task

        idf = E(nc.sbuf_tensor("idf_s", [128, 128], F32))
        idb = E(nc.sbuf_tensor("idb_s", [128, 128], BF16))
        ones_r_bf = E(nc.sbuf_tensor("ones_r_bf_s", [1, 128], BF16))
        ones_r_f = E(nc.sbuf_tensor("ones_r_f_s", [1, 128], F32))
        ones_cell_bf = E(nc.sbuf_tensor("ones_cell_bf_s", [1, 1], BF16))
        ones88 = E(nc.sbuf_tensor("ones88_s", [8, 8], BF16))
        cmat_sb = E(nc.sbuf_tensor("cmat_sb", [48, H], BF16))
        wz_sb = E(nc.sbuf_tensor("wz_sb", [128, KC * 48], BF16))
        bzrow = E(nc.sbuf_tensor("bzrow", [1, 48], F32))
        bzrow_bf = E(nc.sbuf_tensor("bzrow_bf", [1, 48], BF16))
        brow = {"bbase": E(nc.sbuf_tensor("br_bbase", [1, H], F32))}
        bencT = E(nc.sbuf_tensor("bencT", [128, NE * 2], F32))
        bdecT = E(nc.sbuf_tensor("bdecT", [128, NE * KC], F32))
        bdn_raw = E(nc.sbuf_tensor("bdn_raw", [NE, R], F32))
        cidx_i = E(nc.sbuf_tensor("cidx_i", [1, NE], I32))
        cidx_f = E(nc.sbuf_tensor("cidx_f", [1, NE], F32))
        epsc = E(nc.sbuf_tensor("epsc", [128, 1], F32))
        benc_raw = E(nc.sbuf_tensor("benc_raw", [NE, HE], F32))
        bdec_raw = E(nc.sbuf_tensor("bdec_raw", [NE, H], F32))

        PS = [E(nc.psum_tensor(f"ps{j}", [128, 512], F32)) for j in range(6)]
        PT1 = E(nc.psum_tensor("pt1", [128, 1024], BF16))
        PT2 = E(nc.psum_tensor("pt2", [128, 1024], BF16))

        es_a = ExitStack()
        EA = es_a.enter_context
        win_sb = EA(nc.sbuf_tensor("win_sb", [128, KC * H], BF16))
        wd_sb = EA(nc.sbuf_tensor("wd_sb", [128, KC * H], BF16))
        wout_sb = EA(nc.sbuf_tensor("wout_sb", [128, KC * H], BF16))
        gam_bc = EA(nc.sbuf_tensor("gam_bc", [128, H], F32))
        bet_bc = EA(nc.sbuf_tensor("bet_bc", [128, H], F32))
        scanA = EA(nc.sbuf_tensor("scanA", [S, T], BF16))
        scanB = EA(nc.sbuf_tensor("scanB", [S, T], BF16))
        hs_bf = EA(nc.sbuf_tensor("hs_bf", [S, NB * T], BF16))
        brow_bf = {nm: EA(nc.sbuf_tensor(f"brbf_{nm}", [1, H], BF16))
                   for nm in ("bin", "bd", "bout")}
        lnv = [EA(nc.sbuf_tensor(f"lnv{j}", [128, 1], F32)) for j in range(2)]
        xb = [EA(nc.sbuf_tensor(f"xb{j}", [128, H], F32)) for j in range(2)]
        ln1bf = [EA(nc.sbuf_tensor(f"ln1bf{j}", [128, H], BF16)) for j in range(2)]
        ln1t = [EA(nc.sbuf_tensor(f"ln1t{j}", [128, H], BF16)) for j in range(2)]
        zs = [EA(nc.sbuf_tensor(f"zs{j}", [128, 32], BF16)) for j in range(2)]
        zse = [EA(nc.sbuf_tensor(f"zse{j}", [128, 32], F32)) for j in range(2)]
        zbx = [EA(nc.sbuf_tensor(f"zbx{j}", [128, S], BF16)) for j in range(2)]
        zbv = [EA(nc.sbuf_tensor(f"zb{j}", [128, S], BF16)) for j in range(2)]
        bnst = [EA(nc.sbuf_tensor(f"bnst{j}", [128, 12], F32)) for j in range(2)]
        bnag = [EA(nc.sbuf_tensor(f"bnag{j}", [128, 2], F32)) for j in range(2)]
        rstd = [EA(nc.sbuf_tensor(f"rstd{j}", [128, 1], F32)) for j in range(2)]
        nmr = [EA(nc.sbuf_tensor(f"nmr{j}", [128, 1], F32)) for j in range(2)]
        xpf = [EA(nc.sbuf_tensor(f"xpf{j}", [128, H], F32)) for j in range(3)]
        xpb = [EA(nc.sbuf_tensor(f"xpb{j}", [128, H], BF16)) for j in range(2)]
        xpt = [EA(nc.sbuf_tensor(f"xpt{j}", [128, H], BF16)) for j in range(2)]
        ybf = [EA(nc.sbuf_tensor(f"ybf{j}", [128, H], BF16)) for j in range(2)]
        yt = [EA(nc.sbuf_tensor(f"yt{j}", [128, H], BF16)) for j in range(2)]
        y2f = [EA(nc.sbuf_tensor(f"y2f{j}", [128, H], F32)) for j in range(2)]
        vf = [EA(nc.sbuf_tensor(f"vf{j}", [128, H], F32)) for j in range(2)]
        ln2bf = [EA(nc.sbuf_tensor(f"ln2bf{j}", [128, H], BF16)) for j in range(2)]
        ln2t = [EA(nc.sbuf_tensor(f"ln2t{j}", [128, H], BF16)) for j in range(2)]

        es_p0 = ExitStack()
        EP0 = es_p0.enter_context
        wbstage8 = [EP0(nc.sbuf_tensor(f"wbstage8_{j}", [128, H], BF16)) for j in range(2)]
        wtr = [EP0(nc.sbuf_tensor(f"wtr{j}", [128, H], BF16)) for j in range(2)]
        wcat = EP0(nc.sbuf_tensor("wcat", [128, KC * 48], BF16))
        wtmp = EP0(nc.sbuf_tensor("wtmp", [128, KC * S], BF16))
        bmat_sb = EP0(nc.sbuf_tensor("bmat_sb", [128, KC * S], BF16))
        wig_sb = EP0(nc.sbuf_tensor("wig_sb", [128, KC * S], BF16))
        wfg_sb = EP0(nc.sbuf_tensor("wfg_sb", [128, KC * S], BF16))
        bzcol = EP0(nc.sbuf_tensor("bzcol", [S, 3], F32))
        for nm in ("gam", "bet"):
            brow[nm] = EP0(nc.sbuf_tensor(f"br_{nm}", [1, H], F32))
        brow_bf["bb"] = EP0(nc.sbuf_tensor("brbf_bb", [1, H], BF16))
        brow_s_bf = {nm: EP0(nc.sbuf_tensor(f"brsbf_{nm}", [1, S], BF16))
                     for nm in ("big", "bfg")}
        bcol = {nm: EP0(nc.sbuf_tensor(f"bc_{nm}", [128, KC], BF16))
                for nm in ("bin", "bb")}

        L = {}
        for nm, dst, src in [
            ("idf", idf[:], d_idf), ("ones_r_f", ones_r_f[:], d_ones_r_f),
            ("gam", brow["gam"][:], d_gam), ("bet", brow["bet"][:], d_bet),
            ("bbase", brow["bbase"][:], d_bbase),
            ("bdn", bdn_raw[:], d_bdn),
            ("benc", benc_raw[:], d_benc), ("bdec", bdec_raw[:], d_bdec),
            ("cidx", cidx_i[:], d_cidx),
        ]:
            L[nm] = TK("sync", lambda e, dst=dst, src=src:
                       e.dma_start(out=dst, in_=src), dma=True)
        for nm, dst, src in [
            ("binbf", brow_bf["bin"][:], d_bin), ("bbbf", brow_bf["bb"][:], d_bb),
            ("bdbf", brow_bf["bd"][:], d_bd), ("boutbf", brow_bf["bout"][:], d_bout),
            ("bigbf", brow_s_bf["big"][:], d_big),
            ("bfgbf", brow_s_bf["bfg"][:], d_bfg),
            ("idb", idb[:], d_idb), ("ones_r_bf", ones_r_bf[:], d_ones_r_bf),
            ("ones_cell", ones_cell_bf[:], d_ones_cell_bf),
            ("ones88", ones88[:], d_ones88),
            ("win", win_sb[:], d_win.rearrange("(c p) f -> p c f", p=128)),
            ("wd", wd_sb[:], d_wd.rearrange("(c p) f -> p c f", p=128)),
            ("wout", wout_sb[:], d_wout.rearrange("(c p) f -> p c f", p=128)),
            ("bmat", bmat_sb[:], d_bmat.rearrange("(c p) s -> p c s", p=128)),
            ("wig", wig_sb[:], d_wig.rearrange("(c p) s -> p c s", p=128)),
            ("wfg", wfg_sb[:], d_wfg.rearrange("(c p) s -> p c s", p=128)),
            ("cmat", cmat_sb[0:S, :], d_cmat),
        ]:
            L[nm] = TK("gpsimd", lambda e, dst=dst, src=src:
                       e.dma_start(out=dst, in_=src), dma=True)

        evBW = []
        prev_bw_mm = [None, None]
        prev_lwb = [None, None]
        for rc in range(KC):
            p = rc % 2
            PTx = PT1 if p == 0 else PT2
            bk = 6 + p
            lwb = TK("gpsimd", lambda e, rc=rc, p=p: e.dma_start(
                out=wbstage8[p][:], in_=d_wb[rc * 128:(rc + 1) * 128, :]),
                deps=[prev_bw_mm[p]], dma=True)
            t_tr = TK("tensor", lambda e, p=p, PTx=PTx: [
                e.transpose(PTx[:, b_ * 128:(b_ + 1) * 128],
                            wbstage8[p][:, b_ * 128:(b_ + 1) * 128], idb[:])
                for b_ in range(KC)][-1],
                deps=[lwb, L["idb"]] + sch.bank_deps([bk]))
            sch.bank_wrote([bk])
            v_ev = TK("vector", lambda e, p=p, PTx=PTx:
                      e.tensor_copy(wtr[p][:], PTx[:]), deps=[t_tr])
            sch.bank_read([bk], v_ev)
            def mm(e, rc=rc, p=p):
                for k in range(KC):
                    last = e.matmul(PS[0][0:128, 0:S],
                                    wtr[p][:, k * 128:(k + 1) * 128],
                                    bmat_sb[:, k * S:(k + 1) * S],
                                    start=(k == 0), stop=(k == KC - 1))
                return last
            t = TK("tensor", mm, deps=[v_ev, L["bmat"]] + sch.bank_deps([0]))
            prev_bw_mm[p] = t
            sch.bank_wrote([0])
            sv = TK("scalar", lambda e, rc=rc:
                    e.activation(wtmp[:, rc * S:(rc + 1) * S], PS[0][0:128, 0:S],
                                 AF.Copy), deps=[t])
            sch.bank_read([0], sv)
            evBW.append(sv)

        def wcat_fn(e):
            for c in range(KC):
                e.tensor_copy(wcat[:, c * 48:c * 48 + S], wfg_sb[:, c * S:(c + 1) * S])
                e.tensor_copy(wcat[:, c * 48 + S:c * 48 + 2 * S],
                              wig_sb[:, c * S:(c + 1) * S])
                last = e.tensor_copy(wcat[:, c * 48 + 2 * S:c * 48 + 3 * S],
                                     wtmp[:, c * S:(c + 1) * S])
            return last
        t_wcat = TK("vector", wcat_fn, deps=[L["wfg"], L["wig"]] + evBW)

        evWz = []
        prev_wz_mm = [None, None]
        for rc in range(KC):
            p = rc % 2
            PTx = PT1 if p == 0 else PT2
            bk = 6 + p
            t_tr = TK("tensor", lambda e, rc=rc, p=p, PTx=PTx: [
                e.transpose(PTx[:, b_ * 128:(b_ + 1) * 128],
                            win_sb[:, rc * H + b_ * 128:rc * H + b_ * 128 + 128],
                            idb[:])
                for b_ in range(KC)][-1],
                deps=[L["win"], L["idb"], prev_wz_mm[p]] + sch.bank_deps([bk]))
            sch.bank_wrote([bk])
            v_ev = TK("vector", lambda e, p=p, PTx=PTx:
                      e.tensor_copy(wtr[p][:], PTx[:]), deps=[t_tr])
            sch.bank_read([bk], v_ev)
            def mm(e, rc=rc, p=p):
                for k in range(KC):
                    last = e.matmul(PS[1][0:128, 0:48],
                                    wtr[p][:, k * 128:(k + 1) * 128],
                                    wcat[:, k * 48:(k + 1) * 48],
                                    start=(k == 0), stop=(k == KC - 1))
                return last
            t = TK("tensor", mm, deps=[v_ev, t_wcat] + sch.bank_deps([1]))
            prev_wz_mm[p] = t
            sch.bank_wrote([1])
            sv = TK("scalar", lambda e, rc=rc:
                    e.activation(wz_sb[:, rc * 48:(rc + 1) * 48], PS[1][0:128, 0:48],
                                 AF.Copy), deps=[t])
            sch.bank_read([1], sv)
            evWz.append(sv)

        t_bfr = None
        t_cidx = TK("vector", lambda e: e.tensor_copy(cidx_f[:], cidx_i[:]),
                    deps=[L["cidx"]])
        t_eps = TK("vector", lambda e: e.memset(epsc[:], float(EPS)), deps=[])
        def bcols_fn(e):
            for ci, nm in enumerate(("bin", "bb")):
                for c in range(KC):
                    last = e.matmul(PS[2][0:128, ci * KC + c:ci * KC + c + 1],
                                    brow_bf[nm][0:1, c * 128:(c + 1) * 128],
                                    ones_cell_bf[:], start=True, stop=True)
            return last
        t_bcols = TK("tensor", bcols_fn,
                     deps=[L["binbf"], L["bbbf"], L["ones_cell"]]
                     + sch.bank_deps([2]))
        sch.bank_wrote([2])
        t_evbc = TK("scalar", lambda e: (
            e.activation(bcol["bin"][:], PS[2][0:128, 0:KC], AF.Copy),
            e.activation(bcol["bb"][:], PS[2][0:128, KC:2 * KC], AF.Copy))[-1],
            deps=[t_bcols])
        sch.bank_read([2], t_evbc)

        def bz_fn(e):
            for col, wsb, bapp in ((0, wfg_sb, brow_s_bf["bfg"]),
                                   (1, wig_sb, brow_s_bf["big"])):
                for c in range(KC):
                    e.matmul(PS[3][0:S, col:col + 1], wsb[:, c * S:(c + 1) * S],
                             bcol["bin"][:, c:c + 1], start=(c == 0), stop=False)
                e.matmul(PS[3][0:S, col:col + 1], bapp[:], ones_cell_bf[:],
                         start=False, stop=True)
            for c in range(KC):
                e.matmul(PS[3][0:S, 2:3], wtmp[:, c * S:(c + 1) * S],
                         bcol["bin"][:, c:c + 1], start=(c == 0), stop=False)
            for c in range(KC):
                last = e.matmul(PS[3][0:S, 2:3], bmat_sb[:, c * S:(c + 1) * S],
                                bcol["bb"][:, c:c + 1], start=False,
                                stop=(c == KC - 1))
            return last
        t_bz = TK("tensor", bz_fn,
                  deps=[t_evbc, L["bigbf"], L["bfgbf"], L["bmat"]]
                  + evBW + sch.bank_deps([3]))
        sch.bank_wrote([3])
        t_evbz = TK("scalar", lambda e: e.activation(bzcol[:], PS[3][0:S, 0:3],
                                                     AF.Copy), deps=[t_bz])
        sch.bank_read([3], t_evbz)
        t_bzflip = [TK("gpsimd", lambda e, g=g: e.dma_start(
            out=bzrow[0:1, g * S:(g + 1) * S], in_=bzcol[0:S, g:g + 1],
            allow_slow_non_contiguous=True), deps=[t_evbz], dma=True)
            for g in range(3)]
        t_bzbf = TK("vector", lambda e: e.tensor_copy(bzrow_bf[:], bzrow[:]),
                    deps=t_bzflip)

        evgb = []
        for half in range(2):
            def gb(e, half=half):
                e.matmul(PS[4][:, 0:512], ones_r_f[:],
                         brow["gam"][0:1, half * 512:half * 512 + 512],
                         start=True, stop=True)
                return e.matmul(PS[5][:, 0:512], ones_r_f[:],
                                brow["bet"][0:1, half * 512:half * 512 + 512],
                                start=True, stop=True)
            t = TK("tensor", gb, deps=[L["gam"], L["bet"], L["ones_r_f"]]
                   + sch.bank_deps([4, 5]))
            sch.bank_wrote([4, 5])
            v = TK("vector", lambda e, half=half: (
                e.tensor_copy(gam_bc[:, half * 512:half * 512 + 512], PS[4][:, 0:512]),
                e.tensor_copy(bet_bc[:, half * 512:half * 512 + 512],
                              PS[5][:, 0:512]))[-1], deps=[t])
            sch.bank_read([4, 5], v)
            evgb.append(v)

        def btr_fn(e):
            for c in range(2):
                e.matmul(PS[2][0:128, 16 + 80 + c * NE:16 + 80 + (c + 1) * NE],
                         benc_raw[0:NE, c * 128:(c + 1) * 128],
                         idf[0:NE, 0:NE], start=True, stop=True)
            for c in range(KC):
                last = e.matmul(
                    PS[2][0:128, 16 + c * NE:16 + (c + 1) * NE],
                    bdec_raw[0:NE, c * 128:(c + 1) * 128], idf[0:NE, 0:NE],
                    start=True, stop=True)
            return last
        t_btr = TK("tensor", btr_fn, deps=[L["benc"], L["bdec"], L["idf"]]
                   + sch.bank_deps([2]))
        sch.bank_wrote([2])
        t_evbtr0 = TK("scalar", lambda e:
                      e.activation(bencT[:], PS[2][0:128, 96:96 + 16], AF.Copy),
                      deps=[t_btr])
        t_evbtr = TK("vector", lambda e: e.tensor_copy(
            bdecT[:], PS[2][0:128, 16:16 + NE * KC]), deps=[t_btr, t_evbtr0])
        sch.bank_read([2], t_evbtr0)
        sch.bank_read([2], t_evbtr)

        endP0 = sch.fence() + [t_bzbf] + evWz

        ln_last_t4 = [None, None]

        def ln_stats(src, bi, dep):
            t1 = TK("vector", lambda e: (
                e.bn_stats(bnst[bi][:, 0:6], src[:, 0:512]),
                e.bn_stats(bnst[bi][:, 6:12], src[:, 512:1024]))[-1],
                deps=[dep, ln_last_t4[bi]])
            t2 = TK("vector", lambda e: e.bn_aggr(bnag[bi][:], bnst[bi][:]),
                    deps=[t1])
            t3 = TK("scalar", lambda e: e.activation(lnv[bi][:], bnag[bi][:, 1:2],
                                                     AF.Ln, bias=epsc[:]),
                    deps=[t2, t_eps])
            t4 = TK("scalar", lambda e: e.activation(rstd[bi][:], lnv[bi][:],
                                                     AF.Exp, scale=-0.5), deps=[t3])
            ln_last_t4[bi] = t4
            return t4

        sch.phase = "p1"
        V_norm1 = [None] * NT
        G_sp1 = [None] * NT
        T_z_l = [None] * NT
        V_evt1 = [None] * NT
        V_scan = [None] * NB
        V_hsb = [None] * NB
        S_eA = [None] * NT
        V_eB = [None] * NT
        RS1 = [None] * NT
        T_tr1 = [None] * NT
        Y_x1l_ = [None] * (NT + 1)

        def p1_load(i):
            bi = i % 2
            return TK("sync", lambda e, i=i, bi=bi: e.dma_start(
                out=xb[bi][:], in_=d_x[i * 128:(i + 1) * 128, :]),
                deps=[V_norm1[i - 2] if i >= 2 else None], dma=True)

        Y_x1l_[0] = p1_load(0)
        for it in range(NT + 1):
            if it < NT:
                i = it
                bi = i % 2
                RS1[i] = ln_stats(xb[bi], bi, Y_x1l_[i])
            if it >= 1:
                i = it - 1
                b, j = divmod(i, TPB)
                bi = i % 2
                p = i % 2
                PTx = PT1 if p == 0 else PT2
                bk = 6 + p
                V_norm1[i] = TK("vector", lambda e, bi=bi: e.tensor_scalar(
                    ln1bf[bi][:], xb[bi][:], bnag[bi][:, 0:1], rstd[bi][:],
                    ALU.subtract, ALU.mult),
                    deps=[RS1[i], T_tr1[i - 2] if i >= 2 else None])
                t_tr = TK("tensor", lambda e, bi=bi, PTx=PTx: [
                    e.transpose(PTx[:, c * 128:(c + 1) * 128],
                                ln1bf[bi][:, c * 128:(c + 1) * 128], idb[:])
                    for c in range(KC)][-1],
                    deps=[V_norm1[i]] + sch.bank_deps([bk]))
                T_tr1[i] = t_tr
                sch.bank_wrote([bk])
                V_evt1[i] = TK("vector", lambda e, bi=bi, PTx=PTx:
                               e.tensor_copy(ln1t[bi][:], PTx[:]),
                               deps=[t_tr, G_sp1[i - 2] if i >= 2 else None])
                sch.bank_read([bk], V_evt1[i])
                zbank = bi
                def zmm(e, bi=bi, zbank=zbank):
                    for c in range(KC):
                        e.matmul(PS[zbank][:, 0:48],
                                 ln1t[bi][:, c * 128:(c + 1) * 128],
                                 wz_sb[:, c * 48:(c + 1) * 48],
                                 start=(c == 0), stop=False)
                    return e.matmul(PS[zbank][:, 0:48], ones_r_bf[:], bzrow_bf[:],
                                    start=False, stop=True)
                T_z_l[i] = TK("tensor", zmm,
                              deps=[V_evt1[i], evWz[-1], t_bzbf]
                              + sch.bank_deps([zbank]))
                sch.bank_wrote([zbank])
                s_exp = TK("scalar", lambda e, bi=bi, zbank=zbank: e.activation(
                    zse[bi][:], PS[zbank][:, 0:32], AF.Exp, scale=-1.0),
                    deps=[T_z_l[i]])
                v_sig1 = TK("vector", lambda e, bi=bi: e.tensor_scalar(
                    zse[bi][:], zse[bi][:], 1.0, None, ALU.add, ALU.bypass),
                    deps=[s_exp])
                def _recip(e, bi=bi):
                    with nc.allow_low_precision(reason="bf16 sigmoid output"):
                        return e.reciprocal(zs[bi][:], zse[bi][:])
                s_sig = TK("vector", _recip, deps=[v_sig1])
                s_bx = TK("scalar", lambda e, bi=bi, zbank=zbank: e.activation(
                    zbx[bi][:], PS[zbank][:, 32:48], AF.Copy),
                    deps=[T_z_l[i], s_exp])
                v_zb = TK("vector", lambda e, bi=bi: e.tensor_tensor(
                    zbv[bi][:], zbx[bi][:], zs[bi][:, 16:32], ALU.mult),
                    deps=[s_bx, s_sig])
                sch.bank_read([zbank], s_exp)
                sch.bank_read([zbank], s_bx)
                PTo = PT2 if p == 0 else PT1
                bko = 7 - p
                t_zt = TK("tensor", lambda e, bi=bi, PTo=PTo: (
                    e.transpose(PTo[0:32, 0:128], zs[bi][:], idb[:]),
                    e.transpose(PTo[0:16, 128:256], zbv[bi][:], idb[:]))[-1],
                    deps=[s_sig, v_zb] + sch.bank_deps([bko]))
                sch.bank_wrote([bko])
                S_eA[i] = TK("scalar", lambda e, j=j, PTo=PTo: e.activation(
                    scanA[0:S, j * 128:(j + 1) * 128], PTo[0:16, 0:128],
                    AF.Copy),
                    deps=[t_zt, V_scan[b - 1] if b > 0 else None])
                V_eB[i] = TK("vector", lambda e, j=j, PTo=PTo: e.tensor_copy(
                    scanB[0:S, j * 128:(j + 1) * 128], PTo[0:16, 128:256]),
                    deps=[t_zt, S_eA[i], V_hsb[b - 1] if b > 0 else None])
                sch.bank_read([bko], S_eA[i])
                sch.bank_read([bko], V_eB[i])
                G_sp1[i] = TK("gpsimd", lambda e, i=i, bi=bi: e.dma_start(
                    out=d_ln1t[i], in_=ln1t[bi][:]), deps=[V_evt1[i]], dma=True)
                if j == TPB - 1:
                    V_scan[b] = TK("vector", lambda e: e.tensor_tensor_scan(
                        scanB[0:S, :], scanA[0:S, :], scanB[0:S, :], 0.0,
                        ALU.mult, ALU.add), deps=[S_eA[i], V_eB[i]])
                    V_hsb[b] = TK("vector", lambda e, b=b: e.tensor_copy(
                        hs_bf[0:S, b * T:(b + 1) * T], scanB[0:S, :]),
                        deps=[V_scan[b]])
            if it + 1 < NT:
                Y_x1l_[it + 1] = p1_load(it + 1)
        sch.phase = "p2"
        V_y2 = [None] * NT
        S_evxp = [None] * NT
        T_xp_l = [None] * NT
        T_xpT = [None] * NT
        T_y_l = [None] * NT
        T_y2_l = [None] * NT
        V_v = [None] * NT
        V_x1 = [None] * NT
        V_norm2 = [None] * NT
        V_evl2T = [None] * NT
        G_sp2a = [None] * NT
        G_sp2b = [None] * NT
        Yl1 = [None] * NT
        Yx2 = [None] * NT
        V_evxp = [None] * NT
        V_evxpT = [None] * NT
        S_tanh = [None] * NT
        V_evyT = [None] * NT
        T_yT_l = [None] * NT

        def p2_yl1(i):
            bi = i % 2
            return TK("sync", lambda e, i=i, bi=bi: e.dma_start(
                out=ln1t[bi][:], in_=d_ln1t[i]),
                deps=[G_sp1[i], T_xp_l[i - 2] if i >= 2 else T_z_l[NT - 1],
                      G_sp1[NT - 1] if i < 2 else None],
                dma=True)

        Yl1[0] = p2_yl1(0)
        for it in range(NT + 4):
            if it + 1 < NT:
                Yl1[it + 1] = p2_yl1(it + 1)
            if 0 <= it - 3 < NT:
                i = it - 3
                bi = i % 2
                x3 = i % 3
                V_y2[i] = TK("vector", lambda e, bi=bi, x3=x3: (
                    e.scalar_tensor_tensor(y2f[bi][:, 0:512], PS[4][:, 0:512], 1.0,
                                           xpf[x3][:, 0:512], ALU.mult, ALU.add),
                    e.scalar_tensor_tensor(y2f[bi][:, 512:1024], PS[5][:, 0:512],
                                           1.0, xpf[x3][:, 512:1024],
                                           ALU.mult, ALU.add))[-1],
                    deps=[T_y2_l[i], S_evxp[i]])
                sch.bank_read([4, 5], V_y2[i])
            if 0 <= it - 2 < NT:
                i = it - 2
                bi = i % 2
                Yx2[i] = TK("sync", lambda e, i=i, bi=bi: e.dma_start(
                    out=xb[bi][:], in_=d_x[i * 128:(i + 1) * 128, :]),
                    deps=[V_v[i - 2] if i >= 2 else V_norm1[NT - 1]], dma=True)
            if 0 <= it - 1 < NT:
                i = it - 1
                bi = i % 2
                T_xpT[i] = TK("tensor", lambda e, bi=bi: [
                    e.transpose(PT1[:, c * 128:(c + 1) * 128],
                                xpb[bi][:, c * 128:(c + 1) * 128], idb[:])
                    for c in range(KC)][-1],
                    deps=[V_evxp[i]] + sch.bank_deps([6]))
                sch.bank_wrote([6])
                V_evxpT[i] = TK("scalar", lambda e, bi=bi: e.activation(
                    xpt[bi][:], PT1[:], AF.Copy),
                    deps=[T_xpT[i], T_y_l[i - 2] if i >= 2 else None])
                sch.bank_read([6], V_evxpT[i])
            if 0 <= it - 2 < NT:
                i = it - 2
                bi = i % 2
                x3 = i % 3
                T_yT_l[i] = TK("tensor", lambda e, bi=bi: [
                    e.transpose(PT2[:, c * 128:(c + 1) * 128],
                                ybf[bi][:, c * 128:(c + 1) * 128], idb[:])
                    for c in range(KC)][-1],
                    deps=[S_tanh[i]] + sch.bank_deps([7]))
                sch.bank_wrote([7])
                V_evyT[i] = TK("scalar", lambda e, bi=bi: e.activation(
                    yt[bi][:], PT2[:], AF.Copy),
                    deps=[T_yT_l[i], T_y2_l[i - 2] if i >= 2 else None])
                sch.bank_read([7], V_evyT[i])
            if 0 <= it - 3 < NT:
                i = it - 3
                bi = i % 2
                V_v[i] = TK("vector", lambda e, bi=bi: e.tensor_tensor(
                    vf[bi][:], xb[bi][:], bet_bc[:], ALU.add),
                    deps=[Yx2[i], evgb[-1], G_sp2a[i - 2] if i >= 2 else None])
                rs2 = ln_stats(y2f[bi], bi, V_y2[i])
                nmr_t = TK("vector", lambda e, bi=bi: e.tensor_scalar(
                    nmr[bi][:], bnag[bi][:, 0:1], rstd[bi][:], -1.0,
                    ALU.mult, ALU.mult), deps=[rs2])
                v_t1 = TK("scalar", lambda e, bi=bi: e.activation(
                    y2f[bi][:], y2f[bi][:], AF.Identity,
                    bias=nmr[bi][:], scale=rstd[bi][:]),
                    deps=[nmr_t, rs2])
                v_u = TK("vector", lambda e, bi=bi: e.tensor_tensor(
                    y2f[bi][:], y2f[bi][:], gam_bc[:], ALU.mult), deps=[v_t1])
                V_x1[i] = TK("vector", lambda e, bi=bi: e.tensor_tensor(
                    vf[bi][:], y2f[bi][:], vf[bi][:], ALU.add),
                    deps=[v_u, V_v[i]])
                rs3 = ln_stats(vf[bi], bi, V_x1[i])
                nm2_t = TK("vector", lambda e, bi=bi: e.tensor_scalar(
                    nmr[bi][:], bnag[bi][:, 0:1], rstd[bi][:], -1.0,
                    ALU.mult, ALU.mult), deps=[rs3])
                V_norm2[i] = TK("scalar", lambda e, bi=bi: e.activation(
                    ln2bf[bi][:], vf[bi][:], AF.Identity,
                    bias=nmr[bi][:], scale=rstd[bi][:]),
                    deps=[nm2_t, rs3])
            if it < NT:
                i = it
                bi = i % 2
                x3 = i % 3
                def xpmm(e, bi=bi):
                    for half in range(2):
                        for c in range(KC):
                            e.matmul(PS[half][:, 0:512],
                                     ln1t[bi][:, c * 128:(c + 1) * 128],
                                     win_sb[:, c * H + half * 512:c * H + half * 512 + 512],
                                     start=(c == 0), stop=False)
                        last = e.matmul(PS[half][:, 0:512], ones_r_bf[:],
                                        brow_bf["bin"][0:1, half * 512:half * 512 + 512],
                                        start=False, stop=True)
                    return last
                T_xp_l[i] = TK("tensor", xpmm,
                               deps=[Yl1[i], L["binbf"]] + sch.bank_deps([0, 1]))
                sch.bank_wrote([0, 1])
                S_evxp[i] = TK("scalar", lambda e, x3=x3: (
                    e.activation(xpf[x3][:, 0:512], PS[0][:, 0:512], AF.Copy),
                    e.activation(xpf[x3][:, 512:1024], PS[1][:, 0:512], AF.Copy))[-1],
                    deps=[T_xp_l[i], V_y2[i - 3] if i >= 3 else None])
                V_evxp[i] = TK("scalar", lambda e, bi=bi, x3=x3: e.activation(
                    xpb[bi][:], xpf[x3][:], AF.Copy),
                    deps=[S_evxp[i], T_xpT[i - 2] if i >= 2 else None])
                sch.bank_read([0, 1], S_evxp[i])
            if 0 <= it - 4 < NT:
                i = it - 4
                bi = i % 2
                t_l2T = TK("tensor", lambda e, bi=bi: [
                    e.transpose(PT1[:, c * 128:(c + 1) * 128],
                                ln2bf[bi][:, c * 128:(c + 1) * 128], idb[:])
                    for c in range(KC)][-1],
                    deps=[V_norm2[i]] + sch.bank_deps([6]))
                sch.bank_wrote([6])
                V_evl2T[i] = TK("scalar", lambda e, bi=bi: e.activation(
                    ln2t[bi][:], PT1[:], AF.Copy),
                    deps=[t_l2T, G_sp2b[i - 2] if i >= 2 else None])
                sch.bank_read([6], V_evl2T[i])
                G_sp2a[i] = TK("gpsimd", lambda e, i=i, bi=bi: e.dma_start(
                    out=d_x1[i], in_=vf[bi][:]), deps=[V_norm2[i]], dma=True)
                G_sp2b[i] = TK("gpsimd", lambda e, i=i, bi=bi: e.dma_start(
                    out=d_ln2t[i], in_=ln2t[bi][:]), deps=[V_evl2T[i]], dma=True)

            if 0 <= it - 1 < NT:
                i = it - 1
                bi = i % 2
                b, j = divmod(i, TPB)
                def ymm(e, bi=bi, b=b, j=j):
                    for half in range(2):
                        for c in range(KC):
                            e.matmul(PS[2 + half][:, 0:512],
                                     xpt[bi][:, c * 128:(c + 1) * 128],
                                     wd_sb[:, c * H + half * 512:c * H + half * 512 + 512],
                                     start=(c == 0), stop=False)
                        e.matmul(PS[2 + half][:, 0:512],
                                 hs_bf[0:S, b * T + j * 128:b * T + (j + 1) * 128],
                                 cmat_sb[0:S, half * 512:half * 512 + 512],
                                 start=False, stop=False)
                        last = e.matmul(PS[2 + half][:, 0:512], ones_r_bf[:],
                                        brow_bf["bd"][0:1, half * 512:half * 512 + 512],
                                        start=False, stop=True)
                    return last
                T_y_l[i] = TK("tensor", ymm,
                              deps=[V_evxpT[i], V_hsb[b], L["bdbf"], L["cmat"]]
                              + sch.bank_deps([2, 3]))
                sch.bank_wrote([2, 3])
                S_tanh[i] = TK("scalar", lambda e, bi=bi: (
                    e.activation(ybf[bi][:, 0:512], PS[2][:, 0:512], AF.Tanh),
                    e.activation(ybf[bi][:, 512:1024], PS[3][:, 0:512], AF.Tanh))[-1],
                    deps=[T_y_l[i], T_yT_l[i - 2] if i >= 2 else None])
                sch.bank_read([2, 3], S_tanh[i])
            if 0 <= it - 2 < NT:
                i = it - 2
                bi = i % 2
                x3 = i % 3
                def y2mm(e, bi=bi):
                    for half in range(2):
                        for c in range(KC):
                            e.matmul(PS[4 + half][:, 0:512],
                                     yt[bi][:, c * 128:(c + 1) * 128],
                                     wout_sb[:, c * H + half * 512:c * H + half * 512 + 512],
                                     start=(c == 0), stop=False)
                        last = e.matmul(PS[4 + half][:, 0:512], ones_r_bf[:],
                                        brow_bf["bout"][0:1, half * 512:half * 512 + 512],
                                        start=False, stop=True)
                    return last
                T_y2_l[i] = TK("tensor", y2mm,
                               deps=[V_evyT[i], L["boutbf"]] + sch.bank_deps([4, 5]))
                sch.bank_wrote([4, 5])
        endP2 = sch.fence()
        sch.phase = "p3"

        es_p0.close()
        es_a.close()
        es_b = ExitStack()
        EB = es_b.enter_context
        bup_raw = EB(nc.sbuf_tensor("bup_raw", [NE, H], F32))
        sqacc = EB(nc.sbuf_tensor("sqacc", [128, NE * KC * T3PB], F32))
        lossm = EB(nc.sbuf_tensor("lossm", [128, 32], F32))
        lossm_bf = EB(nc.sbuf_tensor("lossm_bf", [128, 32], BF16))
        losse = EB(nc.sbuf_tensor("losse", [32, 128], F32))
        lossc = EB(nc.sbuf_tensor("lossc", [32, 1], F32))
        loss8 = EB(nc.sbuf_tensor("loss8", [1, NE], F32))
        minv = EB(nc.sbuf_tensor("minv", [1, 1], F32))
        oh8 = EB(nc.sbuf_tensor("oh8", [1, NE], F32))
        tmp8 = EB(nc.sbuf_tensor("tmp8", [1, NE], F32))
        tmp8b = EB(nc.sbuf_tensor("tmp8b", [1, NE], F32))
        adp8 = EB(nc.sbuf_tensor("adp8", [1, NE], F32))
        adp8_bf = EB(nc.sbuf_tensor("adp8_bf", [1, NE], BF16))
        c8col = EB(nc.sbuf_tensor("c8col", [NE, 1], F32))
        c_all = EB(nc.sbuf_tensor("c_all", [128, NE], F32))
        wdsel_a = [EB(nc.sbuf_tensor("wdsel0", [128, KC * R], F32))]
        wdsel_bf = [EB(nc.sbuf_tensor(f"wdsel_bf{j}", [128, KC * R], BF16))
                    for j in range(2)]
        wusel_a = [EB(nc.sbuf_tensor("wusel0", [64, H], F32))]
        wusel_bf = [EB(nc.sbuf_tensor(f"wusel_bf{j}", [64, H], BF16))
                    for j in range(2)]
        tmp8r = EB(nc.sbuf_tensor("tmp8r", [NE, R], BF16))
        tmp8h = EB(nc.sbuf_tensor("tmp8h", [NE, H], BF16))
        bsel_row = EB(nc.sbuf_tensor("bsel_row", [1, R + H], F32))
        bselH = EB(nc.sbuf_tensor("bselH", [1, H], F32))
        bsel_bf = [EB(nc.sbuf_tensor(f"bsel_bf{j}", [1, R + H], BF16))
                    for j in range(2)]
        t_lmz = TK("vector", lambda e: e.memset(lossm[:], 0.0), deps=endP2)
        wenc_sb = EB(nc.sbuf_tensor("wenc_sb", [128, NE * KC * HE], BF16))
        wdec_sb = EB(nc.sbuf_tensor("wdec_sb", [128, NE * 2 * H], BF16))
        wbase_sb = EB(nc.sbuf_tensor("wbase_sb", [128, KC * H], BF16))
        wdn_sb = EB(nc.sbuf_tensor("wdn_sb", [128, NE * KC * R], BF16))
        wup_sb = EB(nc.sbuf_tensor("wup_sb", [64, NE * H], BF16))
        l2t3 = [EB(nc.sbuf_tensor(f"l2t3_{j}", [128, 4 * H], BF16)) for j in range(2)]
        hta = [EB(nc.sbuf_tensor(f"hta{j}", [128, 512], BF16)) for j in range(2)]
        htb = [EB(nc.sbuf_tensor(f"htb{j}", [128, 512], BF16)) for j in range(2)]
        dsc = [EB(nc.sbuf_tensor(f"dsc{j}", [128, 512], F32)) for j in range(2)]
        ssc = [EB(nc.sbuf_tensor(f"ssc{j}", [128, 512], F32)) for j in range(2)]
        x1l = [EB(nc.sbuf_tensor(f"x1l{j}", [128, H], F32)) for j in range(3)]
        l2tf = [EB(nc.sbuf_tensor(f"l2tf{j}", [128, H], BF16)) for j in range(3)]
        midb = [EB(nc.sbuf_tensor(f"midb{j}", [128, R], BF16)) for j in range(2)]
        midt = [EB(nc.sbuf_tensor(f"midt{j}", [64, 128], BF16)) for j in range(2)]
        outb = [EB(nc.sbuf_tensor(f"outb{j}", [128, H], F32)) for j in range(2)]

        ffn_load_list = []
        for n in range(NE):
            ffn_load_list.append(
                (f"wenc{n}", wenc_sb[:, n * KC * HE:(n + 1) * KC * HE],
                 d_wenc[n].rearrange("(c p) h -> p c h", p=128)))
            ffn_load_list.append(
                (f"wdec{n}", wdec_sb[:, n * 2 * H:(n + 1) * 2 * H],
                 d_wdec[n].rearrange("(rc p) f -> p rc f", p=128)))
        for n in range(NE):
            ffn_load_list.append(
                (f"wdn{n}", wdn_sb[:, n * KC * R:(n + 1) * KC * R],
                 d_wdn[n].rearrange("(c p) r -> p c r", p=128)))
        ffn_load_list.append(
            ("wbase", wbase_sb[:], d_wbase.rearrange("(c p) f -> p c f", p=128)))
        ffn_load_list.append(
            ("wup", wup_sb[:], d_wup.rearrange("n r f -> r n f")))
        for nm, dst, src in ffn_load_list:
            L[nm] = TK("gpsimd", lambda e, dst=dst, src=src:
                       e.dma_start(out=dst, in_=src), deps=endP0 + endP2, dma=True)
        L["wenc"] = L[f"wenc{NE - 1}"]
        L["wdec"] = L[f"wdec{NE - 1}"]
        L["wdn"] = L[f"wdn{NE - 1}"]
        L["bup"] = TK("sync", lambda e: e.dma_start(out=bup_raw[:], in_=d_bup),
                      deps=endP2, dma=True)

        state = {
            "V_red_prev": None, "prev_sel": {}, "sel_out": {},
            "ht_last_reader": [None, None, None, None],
            "dsc_last_sq": [None, None],
            "tile3_last": {},
        }
        T_out_l = [None] * NT
        V_out_l = [None] * NT
        T_midT_l = [None] * NT
        S_relu2 = [None] * NT
        G_out = [None] * NT
        Yx1_l = [None] * NT
        Y_l2 = [None] * NT3
        S_sq_last_b = [None] * NB

        def emit_p3(b, inject=None):
            S_sq_last = {"scalar": None, "vector": None}

            def emit_dec(p):
                n, ni, kk, k = p["n"], p["ni"], p["kk"], p["k"]
                l2view = p["l2view"]
                last_vdiff = None
                last_dec = None

                def dec_w(e, n=n, ni=ni, d=0):
                    bank = 2 + (d % 4)
                    e.matmul(
                        PS[bank][:, 0:512],
                        wdec_sb[:, (n * 2) * H + d * 128:
                                (n * 2) * H + d * 128 + 128],
                        hta[ni][:], start=True, stop=False)
                    return e.matmul(
                        PS[bank][:, 0:512],
                        wdec_sb[:, (n * 2 + 1) * H + d * 128:
                                (n * 2 + 1) * H + d * 128 + 128],
                        htb[ni][:], start=False, stop=True)
                for d in range(KC):
                    bank = 2 + (d % 4)
                    bde = sch.bank_deps([bank])
                    t_w = TK("tensor", lambda e, f=dec_w, d=d: f(e, d=d),
                             deps=[p["s_reluA"], p["s_reluB"],
                                   L[f"wdec{n}"]] + bde)
                    sch.bank_wrote([bank])
                    di = d % 2
                    v_diff = TK("vector", lambda e, n=n, d=d, di=di,
                                bank=bank, l2view=l2view:
                                e.scalar_tensor_tensor(
                                    dsc[di][:], PS[bank][:, 0:512],
                                    bdecT[:, d * NE + n:d * NE + n + 1],
                                    l2view[:, :, d, :],
                                    ALU.add, ALU.subtract),
                                deps=[t_w, state["dsc_last_sq"][di],
                                      t_evbtr])
                    sch.bank_read([bank], v_diff)
                    col = n * KC * T3PB + kk * KC + d
                    red_dep = (state["V_red_prev"]
                               if (n == 0 and d == 0 and kk == 0)
                               else None)
                    if d % 4 != 3:
                        s_sq = TK("scalar", lambda e, di=di, col=col:
                                  e.activation(
                                      ssc[di][:], dsc[di][:], AF.Square,
                                      accum_out=sqacc[:, col:col + 1]),
                                  deps=[v_diff, red_dep])
                    else:
                        s_sq = TK("vector", lambda e, di=di, col=col:
                                  e.scalar_tensor_tensor(
                                      ssc[di][:], dsc[di][:], 1.0,
                                      dsc[di][:], ALU.mult, ALU.mult,
                                      accum_out=sqacc[:, col:col + 1]),
                                  deps=[v_diff, red_dep])
                    state["dsc_last_sq"][di] = s_sq
                    S_sq_last["scalar" if d % 4 != 3 else "vector"] = s_sq
                    last_vdiff = v_diff
                    last_dec = t_w
                state["ht_last_reader"][ni * 2] = last_dec
                state["ht_last_reader"][ni * 2 + 1] = last_dec
                if n == NE - 1:
                    state["tile3_last"][k] = [last_dec, last_vdiff]

            pend = None
            for kk in range(T3PB):
                if kk == 1 and inject is not None:
                    inject()
                    inject = None
                k = b * T3PB + kk
                ki = k % 2
                Y_l2[k] = TK("sync", lambda e, k=k, ki=ki: e.dma_start(
                    out=l2t3[ki][:],
                    in_=d_ln2t[4 * k:4 * k + 4].rearrange("t p f -> p t f")),
                    deps=[G_sp2b[4 * k + 3]]
                    + (state["tile3_last"].pop(k - 2, []) if k >= 2 else [])
                    + (endP2 if k == 0 else []), dma=True)
                l2view = l2t3[ki][:].rearrange("p (t c f) -> p t c f", t=4, c=KC)
                for n in range(NE):
                    ni = n % 2

                    def enc_fn(e, n=n, l2view=l2view, hh=0):
                        for c in range(KC):
                            last = e.matmul(
                                PS[hh][:, 0:512],
                                wenc_sb[:, (n * KC + c) * HE + hh * 128:
                                        (n * KC + c) * HE + hh * 128 + 128],
                                l2view[:, :, c, :],
                                start=(c == 0), stop=(c == KC - 1))
                        return last
                    t_encA = TK("tensor",
                                lambda e, f=enc_fn: f(e, hh=0),
                                deps=[Y_l2[k], L[f"wenc{n}"],
                                      state["ht_last_reader"][ni * 2]]
                                + sch.bank_deps([0]))
                    sch.bank_wrote([0])
                    s_reluA = TK("scalar", lambda e, n=n, ni=ni: e.activation(
                        hta[ni][:], PS[0][:, 0:512], AF.Relu,
                        bias=bencT[:, n:n + 1]),
                        deps=[t_encA, state["ht_last_reader"][ni * 2], t_evbtr])
                    sch.bank_read([0], s_reluA)
                    t_encB = TK("tensor",
                                lambda e, f=enc_fn: f(e, hh=1),
                                deps=[Y_l2[k], L[f"wenc{n}"],
                                      state["ht_last_reader"][ni * 2 + 1]]
                                + sch.bank_deps([1]))
                    sch.bank_wrote([1])
                    s_reluB = TK("scalar", lambda e, n=n, ni=ni: e.activation(
                        htb[ni][:], PS[1][:, 0:512], AF.Relu,
                        bias=bencT[:, NE + n:NE + n + 1]),
                        deps=[t_encB, state["ht_last_reader"][ni * 2 + 1], t_evbtr])
                    sch.bank_read([1], s_reluB)
                    newp = {"n": n, "ni": ni, "kk": kk, "k": k,
                            "l2view": l2view, "s_reluA": s_reluA,
                            "s_reluB": s_reluB}
                    if pend is not None:
                        emit_dec(pend)
                    pend = newp
            emit_dec(pend)
            if inject is not None:
                inject()
            S_sq_last_b[b] = S_sq_last

        def emit_sel_a(b):
            prev_sel = state["prev_sel"]
            V_red = TK("vector", lambda e: [
                e.tensor_reduce(lossm[:, n:n + 1],
                                sqacc[:, n * KC * T3PB:(n + 1) * KC * T3PB],
                                AX.X, ALU.add) for n in range(NE)][-1],
                deps=[S_sq_last_b[b]["scalar"], S_sq_last_b[b]["vector"],
                      t_lmz, prev_sel.get("t_ltr")])
            state["V_red_prev"] = V_red
            v_lmbf = TK("vector", lambda e: e.tensor_copy(lossm_bf[:], lossm[:]),
                        deps=[V_red, prev_sel.get("t_ltr")])
            t_ltr = TK("tensor", lambda e: e.transpose(PT2[0:32, 0:128],
                                                       lossm_bf[:], idb[:]),
                       deps=[v_lmbf] + sch.bank_deps([7]))
            sch.bank_wrote([7])
            state["sel_mid"] = {"prev": prev_sel, "t_ltr": t_ltr, "b": b}

        def emit_sel_rest(b):
            mid = state.pop("sel_mid")
            assert mid["b"] == b
            prev_sel = mid["prev"]
            t_ltr = mid["t_ltr"]
            s_elt = TK("scalar", lambda e: e.activation(losse[:],
                                                        PT2[0:32, 0:128], AF.Copy),
                       deps=[t_ltr, prev_sel.get("v_lred")])
            sch.bank_read([7], s_elt)
            v_lred = TK("vector", lambda e: e.tensor_reduce(lossc[:], losse[:],
                                                            AX.X, ALU.add),
                        deps=[s_elt, prev_sel.get("g_lflip")])
            g_lflip = TK("gpsimd", lambda e: e.dma_start(
                out=loss8[:], in_=lossc[0:NE, 0:1],
                allow_slow_non_contiguous=True),
                deps=[v_lred, prev_sel.get("v_oh"), prev_sel.get("v_adpbf")],
                dma=True)
            v_min = TK("vector", lambda e: e.tensor_reduce(minv[:], loss8[:],
                                                           AX.X, ALU.min),
                       deps=[g_lflip])
            v_oh = TK("vector", lambda e: e.tensor_scalar(
                oh8[:], loss8[:], minv[:], None, ALU.is_le, ALU.bypass),
                deps=[v_min])
            prev = v_oh
            for kkk in range(NE):
                t_eq = TK("vector", lambda e, kkk=kkk: e.tensor_scalar(
                    tmp8[:], cidx_f[:], float(kkk), None, ALU.is_equal, ALU.bypass),
                    deps=[prev, t_cidx,
                          prev_sel.get("g_cflip") if kkk == 0 else None])
                t_m = TK("vector", lambda e: e.tensor_tensor(
                    tmp8b[:], tmp8[:], oh8[:], ALU.mult), deps=[t_eq])
                prev = TK("vector", lambda e, kkk=kkk: e.tensor_reduce(
                    adp8[0:1, kkk:kkk + 1], tmp8b[:], AX.X, ALU.add), deps=[t_m])
            v_adpbf = TK("vector", lambda e: e.tensor_copy(adp8_bf[:], adp8[:]),
                         deps=[prev])
            g_cflip = TK("gpsimd", lambda e: e.dma_start(
                out=c8col[:], in_=adp8[0:1, 0:NE],
                allow_slow_non_contiguous=True),
                deps=[prev, prev_sel.get("v_tr"), prev_sel.get("v_th")], dma=True)
            t_cbc = TK("tensor", lambda e: e.matmul(
                PS[2][:, 128:128 + NE], ones_r_bf[:], adp8_bf[:],
                start=True, stop=True), deps=[v_adpbf] + sch.bank_deps([2]))
            sch.bank_wrote([2])
            s_ecbc = TK("scalar", lambda e: e.activation(
                c_all[:], PS[2][:, 128:128 + NE], AF.Copy), deps=[t_cbc])
            sch.bank_read([2], s_ecbc)
            prevw = TK("vector", lambda e: e.tensor_scalar(
                wdsel_a[0][:], wdn_sb[:, 0:KC * R], c_all[:, 0:1], None,
                ALU.mult, ALU.bypass),
                deps=[s_ecbc, L["wdn"], prev_sel.get("t_out_last")])
            for n in range(1, NE):
                prevw = TK("vector", lambda e, n=n: e.scalar_tensor_tensor(
                    wdsel_a[0][:], wdn_sb[:, n * KC * R:(n + 1) * KC * R],
                    c_all[:, n:n + 1], wdsel_a[0][:], ALU.mult, ALU.add),
                    deps=[prevw])
            v_wdbf = TK("vector", lambda e: e.tensor_copy(
                wdsel_bf[b % 2][:], wdsel_a[0][:]), deps=[prevw])
            prevu = TK("vector", lambda e: e.tensor_scalar(
                wusel_a[0][:], wup_sb[0:64, 0:H], c_all[0:64, 0:1], None,
                ALU.mult, ALU.bypass),
                deps=[s_ecbc, L["wup"], prev_sel.get("t_out_last")])
            for n in range(1, NE):
                prevu = TK("vector", lambda e, n=n: e.scalar_tensor_tensor(
                    wusel_a[0][:], wup_sb[0:64, n * H:(n + 1) * H],
                    c_all[0:64, n:n + 1], wusel_a[0][:], ALU.mult, ALU.add),
                    deps=[prevu])
            v_wubf = TK("vector", lambda e: e.tensor_copy(
                wusel_bf[b % 2][:], wusel_a[0][:]), deps=[prevu])
            v_tr = TK("vector", lambda e: e.tensor_scalar(
                tmp8r[:], bdn_raw[:], c8col[:], None, ALU.mult, ALU.bypass),
                deps=[g_cflip, L["bdn"], prev_sel.get("t_bh_last")])
            v_th = TK("vector", lambda e: e.tensor_scalar(
                tmp8h[:], bup_raw[:], c8col[:], None, ALU.mult, ALU.bypass),
                deps=[g_cflip, L["bup"], prev_sel.get("t_bh_last")])
            t_bselR = TK("tensor", lambda e: e.matmul(
                PS[2][0:8, 256:256 + R], ones88[:], tmp8r[:], start=True, stop=True),
                deps=[v_tr, L["ones88"]] + sch.bank_deps([2]))
            sch.bank_wrote([2])
            s_ebR = TK("scalar", lambda e: e.activation(
                bsel_row[0:1, 0:R], PS[2][0:1, 256:256 + R], AF.Copy),
                deps=[t_bselR])
            sch.bank_read([2], s_ebR)
            prevh = s_ebR
            t_bh = None
            for half in range(2):
                t_bh = TK("tensor", lambda e, half=half: e.matmul(
                    PS[2][0:8, 0:512], ones88[:],
                    tmp8h[0:NE, half * 512:half * 512 + 512], start=True, stop=True),
                    deps=[v_th, prevh] + sch.bank_deps([2]))
                sch.bank_wrote([2])
                prevh = TK("scalar", lambda e, half=half: e.activation(
                    bsel_row[0:1, R + half * 512:R + half * 512 + 512],
                    PS[2][0:1, 0:512], AF.Copy), deps=[t_bh])
                sch.bank_read([2], prevh)
            v_cmb = TK("vector", lambda e: e.tensor_tensor(
                bselH[:], bsel_row[0:1, R:R + H], brow["bbase"][:], ALU.add),
                deps=[prevh, L["bbase"]])
            v_bselbf = TK("vector", lambda e: (
                e.tensor_copy(bsel_bf[b % 2][0:1, 0:R], bsel_row[0:1, 0:R]),
                e.tensor_copy(bsel_bf[b % 2][0:1, R:R + H], bselH[:]))[-1],
                deps=[v_cmb])
            state["prev_sel"] = {
                "t_ltr": t_ltr, "g_lflip": g_lflip, "v_oh": v_oh,
                "v_adpbf": v_adpbf, "g_cflip": g_cflip, "v_tr": v_tr,
                "v_th": v_th, "t_bh_last": t_bh, "v_lred": v_lred}
            state["sel_out"][b] = (v_wdbf, v_wubf, v_bselbf)

        def emit_f2(b, inject=None):
            v_wdbf, v_wubf, v_bselbf = state["sel_out"][b]
            S_relu2_l = [None] * (TPB + 1)
            t_mid_l = [None] * (TPB + 1)
            for jj in range(TPB + 1):
                if jj == 4 and inject is not None:
                    inject()
                    inject = None
                if jj < TPB:
                    j = jj
                    i = b * TPB + j
                    bi = i % 2
                    ti = i % 3
                    yl2f = TK("sync", lambda e, i=i, ti=ti: e.dma_start(
                        out=l2tf[ti][:], in_=d_ln2t[i]),
                        deps=[G_sp2b[i], T_out_l[i - 3] if i >= 3 else None]
                        + (endP2 if i == 0 else []), dma=True)
                    yx1 = TK("sync", lambda e, i=i, ti=ti: e.dma_start(
                        out=x1l[ti][:], in_=d_x1[i]),
                        deps=[G_sp2a[i], V_out_l[i - 3] if i >= 3 else None]
                        + (endP2 if i == 0 else []), dma=True)
                    Yx1_l[i] = yx1
                    def midmm(e, ti=ti):
                        for c in range(KC):
                            e.matmul(PS[0][:, 0:R],
                                     l2tf[ti][:, c * 128:(c + 1) * 128],
                                     wdsel_bf[b % 2][:, c * R:(c + 1) * R],
                                     start=(c == 0), stop=False)
                        return e.matmul(PS[0][:, 0:R], ones_r_bf[:],
                                        bsel_bf[b % 2][0:1, 0:R],
                                        start=False, stop=True)
                    t_mid = TK("tensor", midmm,
                               deps=[yl2f, v_wdbf, v_bselbf] + sch.bank_deps([0]))
                    sch.bank_wrote([0])
                    t_mid_l[j] = t_mid
                    S_relu2[i] = TK("scalar", lambda e, bi=bi: e.activation(
                        midb[bi][:], PS[0][:, 0:R], AF.Relu),
                        deps=[t_mid, T_midT_l[i - 2] if i >= 2 else None])
                    sch.bank_read([0], S_relu2[i])
                if jj >= 1:
                    j = jj - 1
                    i = b * TPB + j
                    bi = i % 2
                    ti = i % 3
                    T_midT_l[i] = TK("tensor", lambda e, bi=bi: e.transpose(
                        PT1[0:64, 0:128], midb[bi][:], idb[:]),
                        deps=[S_relu2[i]] + sch.bank_deps([6]))
                    sch.bank_wrote([6])
                    v_emt = TK("vector", lambda e, bi=bi: e.tensor_copy(
                        midt[bi][:], PT1[0:64, 0:128]),
                        deps=[T_midT_l[i], T_out_l[i - 2] if i >= 2 else None])
                    sch.bank_read([6], v_emt)
                    def outmm(e, bi=bi, ti=ti):
                        for half in range(2):
                            for c in range(KC):
                                e.matmul(PS[2 + half][:, 0:512],
                                         l2tf[ti][:, c * 128:(c + 1) * 128],
                                         wbase_sb[:, c * H + half * 512:c * H + half * 512 + 512],
                                         start=(c == 0), stop=False)
                            e.matmul(PS[2 + half][:, 0:512], midt[bi][:],
                                     wusel_bf[b % 2][0:64,
                                                     half * 512:half * 512 + 512],
                                     start=False, stop=False)
                            last = e.matmul(PS[2 + half][:, 0:512], ones_r_bf[:],
                                            bsel_bf[b % 2][0:1, R + half * 512:
                                                           R + half * 512 + 512],
                                            start=False, stop=True)
                        return last
                    T_out_l[i] = TK("tensor", outmm,
                                    deps=[v_emt, v_wubf, v_bselbf, L["wbase"]]
                                    + sch.bank_deps([2, 3]))
                    sch.bank_wrote([2, 3])
                    V_out_l[i] = TK("vector", lambda e, bi=bi, ti=ti: (
                        e.scalar_tensor_tensor(outb[bi][:, 0:512], PS[2][:, 0:512],
                                               1.0, x1l[ti][:, 0:512],
                                               ALU.mult, ALU.add),
                        e.scalar_tensor_tensor(outb[bi][:, 512:1024],
                                               PS[3][:, 0:512], 1.0,
                                               x1l[ti][:, 512:1024],
                                               ALU.mult, ALU.add))[-1],
                        deps=[T_out_l[i], Yx1_l[i],
                              G_out[i - 2] if i >= 2 else None])
                    sch.bank_read([2, 3], V_out_l[i])
                    G_out[i] = TK("gpsimd", lambda e, i=i, bi=bi: e.dma_start(
                        out=d_out[i * 128:(i + 1) * 128, :], in_=outb[bi][:]),
                        deps=[V_out_l[i]], dma=True)
            if inject is not None:
                inject()
            state["prev_sel"]["t_out_last"] = T_out_l[b * TPB + TPB - 1]

        emit_p3(0)
        emit_sel_a(0)
        for b in range(1, NB):
            emit_p3(b, inject=(lambda: emit_sel_rest(0)) if b == 1 else None)
            emit_sel_a(b)
            emit_f2(b - 1, inject=lambda bb=b: emit_sel_rest(bb))
        emit_f2(NB - 1)

        with nc.Block("main") as blk:
            sch.emit(blk)

        es_b.close()
    return nc


_NC_CACHE = {}


def kernel(**inputs):
    T, NBtot, NCORES = 2048, 32, 8
    NBC = NBtot // NCORES
    key = (T, NBC)
    if key not in _NC_CACHE:
        _NC_CACHE[key] = build_program(T=T, NB=NBC)
    nc = _NC_CACHE[key]

    g = lambda k: np.asarray(inputs[k])
    x = g("L0_attn_x").astype(np.float32, copy=False)
    common = {
        "win": g("L0_attn_W_in"), "wb": g("L0_attn_W_B"),
        "wd": g("L0_attn_W_D"), "wout": g("L0_attn_W_out"),
        "wbase": g("L0_ffn_W_base"),
        "bmat": g("L0_attn_B_mat"), "cmat": g("L0_attn_C_mat"),
        "wig": g("L0_attn_W_ig"), "wfg": g("L0_attn_W_fg"),
        "wenc": g("L0_ffn_W_enc"), "wdec": g("L0_ffn_W_dec"),
        "wdn": g("L0_ffn_W_down"), "wup": g("L0_ffn_W_up"),
        "bin": g("L0_attn_b_in").reshape(1, H),
        "bb": g("L0_attn_b_B").reshape(1, H),
        "bd": g("L0_attn_b_D").reshape(1, H),
        "bout": g("L0_attn_b_out").reshape(1, H),
        "big": g("L0_attn_b_ig").reshape(1, S),
        "bfg": g("L0_attn_b_fg").reshape(1, S),
        "gam": g("L0_attn_gamma").reshape(1, H),
        "bet": g("L0_attn_beta").reshape(1, H),
        "bbase": g("L0_ffn_b_base").reshape(1, H),
        "benc": g("L0_ffn_b_enc"), "bdec": g("L0_ffn_b_dec"),
        "bdn": g("L0_ffn_b_down"), "bup": g("L0_ffn_b_up"),
        "cidx": g("L0_ffn_connected_idx").reshape(1, NE).astype(np.int32),
        "idf": np.eye(128, dtype=np.float32),
        "idb": np.eye(128, dtype=ml_dtypes.bfloat16),
        "ones_r_bf": np.ones((1, 128), dtype=ml_dtypes.bfloat16),
        "ones_r_f": np.ones((1, 128), dtype=np.float32),
        "ones_cell_bf": np.ones((1, 1), dtype=ml_dtypes.bfloat16),
        "ones88": np.ones((8, 8), dtype=ml_dtypes.bfloat16),
    }
    in_maps = []
    for c in range(NCORES):
        m = dict(common)
        m["x"] = np.ascontiguousarray(x[NBC * c:NBC * (c + 1)].reshape(NBC * T, H))
        in_maps.append(m)
    res = run_bass_kernel_spmd(nc, in_maps, core_ids=list(range(NCORES)))
    out = np.concatenate(
        [res.results[c]["out"].reshape(NBC, T, H) for c in range(NCORES)], axis=0)
    return out.astype(np.float32)


# version 26
# speedup vs baseline: 1.3556x; 1.0032x over previous
import numpy as np
import ml_dtypes
from contextlib import ExitStack

import os
import concourse.bass as bass
import concourse.mybir as mybir
from concourse.bass_utils import run_bass_kernel_spmd

PARANOID = False
FREE = {"p1", "p2", "p3"}

F32 = mybir.dt.float32
BF16 = mybir.dt.bfloat16
I32 = mybir.dt.int32
AF = mybir.ActivationFunctionType
ALU = mybir.AluOpType
AX = mybir.AxisListType

H = 1024
KC = 8
S = 16
NE = 8
HE = 256
R = 64
EPS = 1e-5

ENGINES = ("sync", "gpsimd", "tensor", "vector", "scalar")


class Task:
    __slots__ = ("eng", "cnt")


class Sched:
    def __init__(self, nc, es):
        self.nc = nc
        self.sem = {e: es.enter_context(nc.semaphore(f"q_{e}")) for e in ENGINES}
        self.wbsem = {e: es.enter_context(nc.semaphore(f"wb_{e}"))
                      for e in ("vector", "scalar")}
        self.cnt = {e: 0 for e in ENGINES}
        self.items = {e: [] for e in ENGINES}
        self.bank_readers = {}
        self._last = []
        self.phase = "p0"

    def task(self, eng, emit, deps=(), dma=False):
        inc = 16 if dma else 1
        t = Task()
        t.eng = eng
        self.cnt[eng] += inc
        t.cnt = self.cnt[eng]
        deps = [d for d in deps if d is not None]
        if self.phase not in FREE:
            deps = deps + [p for p in self._last if p is not None]
        self._last = [t]
        self.items[eng].append((deps, emit, inc, t))
        return t

    def fence(self, eng=None):
        if eng is None:
            return [self.fence(e) for e in ENGINES]
        t = Task()
        t.eng = eng
        t.cnt = self.cnt[eng]
        return t

    def bank_deps(self, keys):
        deps = []
        for k in keys:
            deps.extend(self.bank_readers.get(k, ()))
        return deps

    def bank_read(self, keys, task):
        for k in keys:
            self.bank_readers.setdefault(k, []).append(task)

    def bank_wrote(self, keys):
        for k in keys:
            self.bank_readers[k] = []

    def emit(self, blk):
        clocks = {e: {e2: 0 for e2 in ENGINES} for e in ENGINES}
        wb = {e: self.wbsem[e] for e in ("vector", "scalar")}
        wbcnt = {e: 0 for e in ("vector", "scalar")}

        class _Proxy:
            def __init__(self, e):
                self._e = e
                self.insts = []

            def __getattr__(self, k):
                f = getattr(self._e, k)
                if not callable(f) or k in ("wait_ge", "nop"):
                    return f

                def g(*a, **kw):
                    r = f(*a, **kw)
                    if hasattr(r, "then_inc"):
                        self.insts.append(r)
                    return r
                return g

        def mk(name):
            def body(e):
                cl = clocks[name]
                fence = name in wb
                for deps, emitfn, inc, t in self.items[name]:
                    for d in deps:
                        if d.cnt > cl[d.eng]:
                            e.wait_ge(self.sem[d.eng], d.cnt)
                            cl[d.eng] = d.cnt
                    if fence:
                        px = _Proxy(e)
                        last = emitfn(px)
                        if len(px.insts) > 1:
                            for ins in px.insts:
                                ins.then_inc(wb[name], 1)
                                wbcnt[name] += 1
                            e.wait_ge(wb[name], wbcnt[name])
                            e.nop().then_inc(self.sem[name], inc)
                        else:
                            last.then_inc(self.sem[name], inc)
                    else:
                        emitfn(e).then_inc(self.sem[name], inc)
                if name == "gpsimd":
                    e.wait_ge(self.sem[name], self.cnt[name])
            return body

        blk.sync(mk("sync"))
        blk.gpsimd(mk("gpsimd"))
        blk.tensor(mk("tensor"))
        blk.vector(mk("vector"))
        blk.scalar(mk("scalar"))


def build_program(T=2048, NB=4):
    NT = NB * T // 128
    TPB = T // 128
    T3PB = T // 512
    NT3 = NB * T3PB
    assert T % 512 == 0 and TPB >= 2

    nc = bass.Bass("TRN2", target_bir_lowering=False,
                   detect_race_conditions=False, name=f"mkseq{T}x{NB}")

    def din(name, shape, dtype=F32):
        return nc.dram_tensor(name, list(shape), dtype, kind="ExternalInput").ap()

    d_x = din("x", (NB * T, H))
    d_win = din("win", (H, H)); d_wb = din("wb", (H, H))
    d_wd = din("wd", (H, H)); d_wout = din("wout", (H, H))
    d_wbase = din("wbase", (H, H))
    d_bmat = din("bmat", (H, S)); d_cmat = din("cmat", (S, H))
    d_wig = din("wig", (H, S)); d_wfg = din("wfg", (H, S))
    d_wenc = din("wenc", (NE, H, HE)); d_wdec = din("wdec", (NE, HE, H))
    d_wdn = din("wdn", (NE, H, R)); d_wup = din("wup", (NE, R, H))
    d_bin = din("bin", (1, H)); d_bb = din("bb", (1, H))
    d_bd = din("bd", (1, H)); d_bout = din("bout", (1, H))
    d_big = din("big", (1, S)); d_bfg = din("bfg", (1, S))
    d_gam = din("gam", (1, H)); d_bet = din("bet", (1, H))
    d_bbase = din("bbase", (1, H))
    d_benc = din("benc", (NE, HE)); d_bdec = din("bdec", (NE, H))
    d_bdn = din("bdn", (NE, R)); d_bup = din("bup", (NE, H))
    d_cidx = din("cidx", (1, NE), I32)
    d_idf = din("idf", (128, 128)); d_idb = din("idb", (128, 128), BF16)
    d_ones_r_bf = din("ones_r_bf", (1, 128), BF16)
    d_ones_r_f = din("ones_r_f", (1, 128))
    d_ones_cell_bf = din("ones_cell_bf", (1, 1), BF16)
    d_ones88 = din("ones88", (8, 8), BF16)

    d_out = nc.dram_tensor("out", [NB * T, H], F32, kind="ExternalOutput").ap()
    d_ln1t = nc.dram_tensor("sp_ln1t", [NT, 128, H], BF16, kind="Internal").ap()
    d_ln2t = nc.dram_tensor("sp_ln2t", [NT, 128, H], BF16, kind="Internal").ap()
    d_x1 = nc.dram_tensor("sp_x1", [NT, 128, H], F32, kind="Internal").ap()

    es = ExitStack()
    with es:
        E = es.enter_context
        sch = Sched(nc, es)
        TK = sch.task

        idf = E(nc.sbuf_tensor("idf_s", [128, 128], F32))
        idb = E(nc.sbuf_tensor("idb_s", [128, 128], BF16))
        ones_r_bf = E(nc.sbuf_tensor("ones_r_bf_s", [1, 128], BF16))
        ones_r_f = E(nc.sbuf_tensor("ones_r_f_s", [1, 128], F32))
        ones_cell_bf = E(nc.sbuf_tensor("ones_cell_bf_s", [1, 1], BF16))
        ones88 = E(nc.sbuf_tensor("ones88_s", [8, 8], BF16))
        cmat_sb = E(nc.sbuf_tensor("cmat_sb", [48, H], BF16))
        wz_sb = E(nc.sbuf_tensor("wz_sb", [128, KC * 48], BF16))
        bzrow = E(nc.sbuf_tensor("bzrow", [1, 48], F32))
        bzrow_bf = E(nc.sbuf_tensor("bzrow_bf", [1, 48], BF16))
        brow = {"bbase": E(nc.sbuf_tensor("br_bbase", [1, H], F32))}
        bencT = E(nc.sbuf_tensor("bencT", [128, NE * 2], F32))
        bdecT = E(nc.sbuf_tensor("bdecT", [128, NE * KC], F32))
        bdn_raw = E(nc.sbuf_tensor("bdn_raw", [NE, R], F32))
        cidx_i = E(nc.sbuf_tensor("cidx_i", [1, NE], I32))
        cidx_f = E(nc.sbuf_tensor("cidx_f", [1, NE], F32))
        epsc = E(nc.sbuf_tensor("epsc", [128, 1], F32))
        benc_raw = E(nc.sbuf_tensor("benc_raw", [NE, HE], F32))
        bdec_raw = E(nc.sbuf_tensor("bdec_raw", [NE, H], F32))

        PS = [E(nc.psum_tensor(f"ps{j}", [128, 512], F32)) for j in range(6)]
        PT1 = E(nc.psum_tensor("pt1", [128, 1024], BF16))
        PT2 = E(nc.psum_tensor("pt2", [128, 1024], BF16))

        es_a = ExitStack()
        EA = es_a.enter_context
        win_sb = EA(nc.sbuf_tensor("win_sb", [128, KC * H], BF16))
        wd_sb = EA(nc.sbuf_tensor("wd_sb", [128, KC * H], BF16))
        wout_sb = EA(nc.sbuf_tensor("wout_sb", [128, KC * H], BF16))
        gam_bc = EA(nc.sbuf_tensor("gam_bc", [128, H], F32))
        bet_bc = EA(nc.sbuf_tensor("bet_bc", [128, H], F32))
        scanA = EA(nc.sbuf_tensor("scanA", [S, T], BF16))
        scanB = EA(nc.sbuf_tensor("scanB", [S, T], BF16))
        hs_bf = EA(nc.sbuf_tensor("hs_bf", [S, NB * T], BF16))
        brow_bf = {nm: EA(nc.sbuf_tensor(f"brbf_{nm}", [1, H], BF16))
                   for nm in ("bin", "bd", "bout")}
        lnv = [EA(nc.sbuf_tensor(f"lnv{j}", [128, 1], F32)) for j in range(2)]
        xb = [EA(nc.sbuf_tensor(f"xb{j}", [128, H], F32)) for j in range(2)]
        ln1bf = [EA(nc.sbuf_tensor(f"ln1bf{j}", [128, H], BF16)) for j in range(2)]
        ln1t = [EA(nc.sbuf_tensor(f"ln1t{j}", [128, H], BF16)) for j in range(2)]
        zs = [EA(nc.sbuf_tensor(f"zs{j}", [128, 32], BF16)) for j in range(2)]
        zse = [EA(nc.sbuf_tensor(f"zse{j}", [128, 32], F32)) for j in range(2)]
        zbx = [EA(nc.sbuf_tensor(f"zbx{j}", [128, S], BF16)) for j in range(2)]
        zbv = [EA(nc.sbuf_tensor(f"zb{j}", [128, S], BF16)) for j in range(2)]
        bnst = [EA(nc.sbuf_tensor(f"bnst{j}", [128, 12], F32)) for j in range(2)]
        bnag = [EA(nc.sbuf_tensor(f"bnag{j}", [128, 2], F32)) for j in range(2)]
        rstd = [EA(nc.sbuf_tensor(f"rstd{j}", [128, 1], F32)) for j in range(2)]
        nmr = [EA(nc.sbuf_tensor(f"nmr{j}", [128, 1], F32)) for j in range(2)]
        xpf = [EA(nc.sbuf_tensor(f"xpf{j}", [128, H], F32)) for j in range(3)]
        xpb = [EA(nc.sbuf_tensor(f"xpb{j}", [128, H], BF16)) for j in range(2)]
        xpt = [EA(nc.sbuf_tensor(f"xpt{j}", [128, H], BF16)) for j in range(2)]
        ybf = [EA(nc.sbuf_tensor(f"ybf{j}", [128, H], BF16)) for j in range(2)]
        yt = [EA(nc.sbuf_tensor(f"yt{j}", [128, H], BF16)) for j in range(2)]
        y2f = [EA(nc.sbuf_tensor(f"y2f{j}", [128, H], F32)) for j in range(2)]
        vf = [EA(nc.sbuf_tensor(f"vf{j}", [128, H], F32)) for j in range(2)]
        ln2bf = [EA(nc.sbuf_tensor(f"ln2bf{j}", [128, H], BF16)) for j in range(2)]
        ln2t = [EA(nc.sbuf_tensor(f"ln2t{j}", [128, H], BF16)) for j in range(2)]

        es_p0 = ExitStack()
        EP0 = es_p0.enter_context
        wbstage8 = [EP0(nc.sbuf_tensor(f"wbstage8_{j}", [128, H], BF16)) for j in range(2)]
        wtr = [EP0(nc.sbuf_tensor(f"wtr{j}", [128, H], BF16)) for j in range(2)]
        wcat = EP0(nc.sbuf_tensor("wcat", [128, KC * 48], BF16))
        wtmp = EP0(nc.sbuf_tensor("wtmp", [128, KC * S], BF16))
        bmat_sb = EP0(nc.sbuf_tensor("bmat_sb", [128, KC * S], BF16))
        wig_sb = EP0(nc.sbuf_tensor("wig_sb", [128, KC * S], BF16))
        wfg_sb = EP0(nc.sbuf_tensor("wfg_sb", [128, KC * S], BF16))
        bzcol = EP0(nc.sbuf_tensor("bzcol", [S, 3], F32))
        for nm in ("gam", "bet"):
            brow[nm] = EP0(nc.sbuf_tensor(f"br_{nm}", [1, H], F32))
        brow_bf["bb"] = EP0(nc.sbuf_tensor("brbf_bb", [1, H], BF16))
        brow_s_bf = {nm: EP0(nc.sbuf_tensor(f"brsbf_{nm}", [1, S], BF16))
                     for nm in ("big", "bfg")}
        bcol = {nm: EP0(nc.sbuf_tensor(f"bc_{nm}", [128, KC], BF16))
                for nm in ("bin", "bb")}

        L = {}
        for nm, dst, src in [
            ("idf", idf[:], d_idf), ("ones_r_f", ones_r_f[:], d_ones_r_f),
            ("gam", brow["gam"][:], d_gam), ("bet", brow["bet"][:], d_bet),
            ("bbase", brow["bbase"][:], d_bbase),
            ("bdn", bdn_raw[:], d_bdn),
            ("benc", benc_raw[:], d_benc), ("bdec", bdec_raw[:], d_bdec),
            ("cidx", cidx_i[:], d_cidx),
        ]:
            L[nm] = TK("sync", lambda e, dst=dst, src=src:
                       e.dma_start(out=dst, in_=src), dma=True)
        for nm, dst, src in [
            ("binbf", brow_bf["bin"][:], d_bin), ("bbbf", brow_bf["bb"][:], d_bb),
            ("bdbf", brow_bf["bd"][:], d_bd), ("boutbf", brow_bf["bout"][:], d_bout),
            ("bigbf", brow_s_bf["big"][:], d_big),
            ("bfgbf", brow_s_bf["bfg"][:], d_bfg),
            ("idb", idb[:], d_idb), ("ones_r_bf", ones_r_bf[:], d_ones_r_bf),
            ("ones_cell", ones_cell_bf[:], d_ones_cell_bf),
            ("ones88", ones88[:], d_ones88),
            ("win", win_sb[:], d_win.rearrange("(c p) f -> p c f", p=128)),
            ("wd", wd_sb[:], d_wd.rearrange("(c p) f -> p c f", p=128)),
            ("wout", wout_sb[:], d_wout.rearrange("(c p) f -> p c f", p=128)),
            ("bmat", bmat_sb[:], d_bmat.rearrange("(c p) s -> p c s", p=128)),
            ("wig", wig_sb[:], d_wig.rearrange("(c p) s -> p c s", p=128)),
            ("wfg", wfg_sb[:], d_wfg.rearrange("(c p) s -> p c s", p=128)),
            ("cmat", cmat_sb[0:S, :], d_cmat),
        ]:
            L[nm] = TK("gpsimd", lambda e, dst=dst, src=src:
                       e.dma_start(out=dst, in_=src), dma=True)

        evBW = []
        prev_bw_mm = [None, None]
        prev_lwb = [None, None]
        for rc in range(KC):
            p = rc % 2
            PTx = PT1 if p == 0 else PT2
            bk = 6 + p
            lwb = TK("gpsimd", lambda e, rc=rc, p=p: e.dma_start(
                out=wbstage8[p][:], in_=d_wb[rc * 128:(rc + 1) * 128, :]),
                deps=[prev_bw_mm[p]], dma=True)
            t_tr = TK("tensor", lambda e, p=p, PTx=PTx: [
                e.transpose(PTx[:, b_ * 128:(b_ + 1) * 128],
                            wbstage8[p][:, b_ * 128:(b_ + 1) * 128], idb[:])
                for b_ in range(KC)][-1],
                deps=[lwb, L["idb"]] + sch.bank_deps([bk]))
            sch.bank_wrote([bk])
            v_ev = TK("vector", lambda e, p=p, PTx=PTx:
                      e.tensor_copy(wtr[p][:], PTx[:]), deps=[t_tr])
            sch.bank_read([bk], v_ev)
            def mm(e, rc=rc, p=p):
                for k in range(KC):
                    last = e.matmul(PS[0][0:128, 0:S],
                                    wtr[p][:, k * 128:(k + 1) * 128],
                                    bmat_sb[:, k * S:(k + 1) * S],
                                    start=(k == 0), stop=(k == KC - 1))
                return last
            t = TK("tensor", mm, deps=[v_ev, L["bmat"]] + sch.bank_deps([0]))
            prev_bw_mm[p] = t
            sch.bank_wrote([0])
            sv = TK("scalar", lambda e, rc=rc:
                    e.activation(wtmp[:, rc * S:(rc + 1) * S], PS[0][0:128, 0:S],
                                 AF.Copy), deps=[t])
            sch.bank_read([0], sv)
            evBW.append(sv)

        def wcat_fn(e):
            for c in range(KC):
                e.tensor_copy(wcat[:, c * 48:c * 48 + S], wfg_sb[:, c * S:(c + 1) * S])
                e.tensor_copy(wcat[:, c * 48 + S:c * 48 + 2 * S],
                              wig_sb[:, c * S:(c + 1) * S])
                last = e.tensor_copy(wcat[:, c * 48 + 2 * S:c * 48 + 3 * S],
                                     wtmp[:, c * S:(c + 1) * S])
            return last
        t_wcat = TK("vector", wcat_fn, deps=[L["wfg"], L["wig"]] + evBW)

        evWz = []
        prev_wz_mm = [None, None]
        for rc in range(KC):
            p = rc % 2
            PTx = PT1 if p == 0 else PT2
            bk = 6 + p
            t_tr = TK("tensor", lambda e, rc=rc, p=p, PTx=PTx: [
                e.transpose(PTx[:, b_ * 128:(b_ + 1) * 128],
                            win_sb[:, rc * H + b_ * 128:rc * H + b_ * 128 + 128],
                            idb[:])
                for b_ in range(KC)][-1],
                deps=[L["win"], L["idb"], prev_wz_mm[p]] + sch.bank_deps([bk]))
            sch.bank_wrote([bk])
            v_ev = TK("vector", lambda e, p=p, PTx=PTx:
                      e.tensor_copy(wtr[p][:], PTx[:]), deps=[t_tr])
            sch.bank_read([bk], v_ev)
            def mm(e, rc=rc, p=p):
                for k in range(KC):
                    last = e.matmul(PS[1][0:128, 0:48],
                                    wtr[p][:, k * 128:(k + 1) * 128],
                                    wcat[:, k * 48:(k + 1) * 48],
                                    start=(k == 0), stop=(k == KC - 1))
                return last
            t = TK("tensor", mm, deps=[v_ev, t_wcat] + sch.bank_deps([1]))
            prev_wz_mm[p] = t
            sch.bank_wrote([1])
            sv = TK("scalar", lambda e, rc=rc:
                    e.activation(wz_sb[:, rc * 48:(rc + 1) * 48], PS[1][0:128, 0:48],
                                 AF.Copy), deps=[t])
            sch.bank_read([1], sv)
            evWz.append(sv)

        t_bfr = None
        t_cidx = TK("vector", lambda e: e.tensor_copy(cidx_f[:], cidx_i[:]),
                    deps=[L["cidx"]])
        t_eps = TK("vector", lambda e: e.memset(epsc[:], float(EPS)), deps=[])
        def bcols_fn(e):
            for ci, nm in enumerate(("bin", "bb")):
                for c in range(KC):
                    last = e.matmul(PS[2][0:128, ci * KC + c:ci * KC + c + 1],
                                    brow_bf[nm][0:1, c * 128:(c + 1) * 128],
                                    ones_cell_bf[:], start=True, stop=True)
            return last
        t_bcols = TK("tensor", bcols_fn,
                     deps=[L["binbf"], L["bbbf"], L["ones_cell"]]
                     + sch.bank_deps([2]))
        sch.bank_wrote([2])
        t_evbc = TK("scalar", lambda e: (
            e.activation(bcol["bin"][:], PS[2][0:128, 0:KC], AF.Copy),
            e.activation(bcol["bb"][:], PS[2][0:128, KC:2 * KC], AF.Copy))[-1],
            deps=[t_bcols])
        sch.bank_read([2], t_evbc)

        def bz_fn(e):
            for col, wsb, bapp in ((0, wfg_sb, brow_s_bf["bfg"]),
                                   (1, wig_sb, brow_s_bf["big"])):
                for c in range(KC):
                    e.matmul(PS[3][0:S, col:col + 1], wsb[:, c * S:(c + 1) * S],
                             bcol["bin"][:, c:c + 1], start=(c == 0), stop=False)
                e.matmul(PS[3][0:S, col:col + 1], bapp[:], ones_cell_bf[:],
                         start=False, stop=True)
            for c in range(KC):
                e.matmul(PS[3][0:S, 2:3], wtmp[:, c * S:(c + 1) * S],
                         bcol["bin"][:, c:c + 1], start=(c == 0), stop=False)
            for c in range(KC):
                last = e.matmul(PS[3][0:S, 2:3], bmat_sb[:, c * S:(c + 1) * S],
                                bcol["bb"][:, c:c + 1], start=False,
                                stop=(c == KC - 1))
            return last
        t_bz = TK("tensor", bz_fn,
                  deps=[t_evbc, L["bigbf"], L["bfgbf"], L["bmat"]]
                  + evBW + sch.bank_deps([3]))
        sch.bank_wrote([3])
        t_evbz = TK("scalar", lambda e: e.activation(bzcol[:], PS[3][0:S, 0:3],
                                                     AF.Copy), deps=[t_bz])
        sch.bank_read([3], t_evbz)
        t_bzflip = [TK("gpsimd", lambda e, g=g: e.dma_start(
            out=bzrow[0:1, g * S:(g + 1) * S], in_=bzcol[0:S, g:g + 1],
            allow_slow_non_contiguous=True), deps=[t_evbz], dma=True)
            for g in range(3)]
        t_bzbf = TK("vector", lambda e: e.tensor_copy(bzrow_bf[:], bzrow[:]),
                    deps=t_bzflip)

        evgb = []
        for half in range(2):
            def gb(e, half=half):
                e.matmul(PS[4][:, 0:512], ones_r_f[:],
                         brow["gam"][0:1, half * 512:half * 512 + 512],
                         start=True, stop=True)
                return e.matmul(PS[5][:, 0:512], ones_r_f[:],
                                brow["bet"][0:1, half * 512:half * 512 + 512],
                                start=True, stop=True)
            t = TK("tensor", gb, deps=[L["gam"], L["bet"], L["ones_r_f"]]
                   + sch.bank_deps([4, 5]))
            sch.bank_wrote([4, 5])
            v = TK("vector", lambda e, half=half: (
                e.tensor_copy(gam_bc[:, half * 512:half * 512 + 512], PS[4][:, 0:512]),
                e.tensor_copy(bet_bc[:, half * 512:half * 512 + 512],
                              PS[5][:, 0:512]))[-1], deps=[t])
            sch.bank_read([4, 5], v)
            evgb.append(v)

        def btr_fn(e):
            for c in range(2):
                e.matmul(PS[2][0:128, 16 + 80 + c * NE:16 + 80 + (c + 1) * NE],
                         benc_raw[0:NE, c * 128:(c + 1) * 128],
                         idf[0:NE, 0:NE], start=True, stop=True)
            for c in range(KC):
                last = e.matmul(
                    PS[2][0:128, 16 + c * NE:16 + (c + 1) * NE],
                    bdec_raw[0:NE, c * 128:(c + 1) * 128], idf[0:NE, 0:NE],
                    start=True, stop=True)
            return last
        t_btr = TK("tensor", btr_fn, deps=[L["benc"], L["bdec"], L["idf"]]
                   + sch.bank_deps([2]))
        sch.bank_wrote([2])
        t_evbtr0 = TK("scalar", lambda e:
                      e.activation(bencT[:], PS[2][0:128, 96:96 + 16], AF.Copy),
                      deps=[t_btr])
        t_evbtr = TK("vector", lambda e: e.tensor_copy(
            bdecT[:], PS[2][0:128, 16:16 + NE * KC]), deps=[t_btr, t_evbtr0])
        sch.bank_read([2], t_evbtr0)
        sch.bank_read([2], t_evbtr)

        endP0 = sch.fence() + [t_bzbf] + evWz

        ln_last_t4 = [None, None]

        def ln_stats(src, bi, dep):
            t1 = TK("vector", lambda e: (
                e.bn_stats(bnst[bi][:, 0:6], src[:, 0:512]),
                e.bn_stats(bnst[bi][:, 6:12], src[:, 512:1024]))[-1],
                deps=[dep, ln_last_t4[bi]])
            t2 = TK("vector", lambda e: e.bn_aggr(bnag[bi][:], bnst[bi][:]),
                    deps=[t1])
            t3 = TK("scalar", lambda e: e.activation(lnv[bi][:], bnag[bi][:, 1:2],
                                                     AF.Ln, bias=epsc[:]),
                    deps=[t2, t_eps])
            t4 = TK("scalar", lambda e: e.activation(rstd[bi][:], lnv[bi][:],
                                                     AF.Exp, scale=-0.5), deps=[t3])
            ln_last_t4[bi] = t4
            return t4

        sch.phase = "p1"
        V_norm1 = [None] * NT
        G_sp1 = [None] * NT
        T_z_l = [None] * NT
        V_evt1 = [None] * NT
        V_scan = [None] * NB
        V_hsb = [None] * NB
        S_eA = [None] * NT
        V_eB = [None] * NT
        RS1 = [None] * NT
        T_tr1 = [None] * NT
        Y_x1l_ = [None] * (NT + 1)

        def p1_load(i):
            bi = i % 2
            return TK("sync", lambda e, i=i, bi=bi: e.dma_start(
                out=xb[bi][:], in_=d_x[i * 128:(i + 1) * 128, :]),
                deps=[V_norm1[i - 2] if i >= 2 else None], dma=True)

        Y_x1l_[0] = p1_load(0)
        for it in range(NT + 1):
            if it < NT:
                i = it
                bi = i % 2
                RS1[i] = ln_stats(xb[bi], bi, Y_x1l_[i])
            if it >= 1:
                i = it - 1
                b, j = divmod(i, TPB)
                bi = i % 2
                p = i % 2
                PTx = PT1 if p == 0 else PT2
                bk = 6 + p
                V_norm1[i] = TK("vector", lambda e, bi=bi: e.tensor_scalar(
                    ln1bf[bi][:], xb[bi][:], bnag[bi][:, 0:1], rstd[bi][:],
                    ALU.subtract, ALU.mult),
                    deps=[RS1[i], T_tr1[i - 2] if i >= 2 else None])
                t_tr = TK("tensor", lambda e, bi=bi, PTx=PTx: [
                    e.transpose(PTx[:, c * 128:(c + 1) * 128],
                                ln1bf[bi][:, c * 128:(c + 1) * 128], idb[:])
                    for c in range(KC)][-1],
                    deps=[V_norm1[i]] + sch.bank_deps([bk]))
                T_tr1[i] = t_tr
                sch.bank_wrote([bk])
                V_evt1[i] = TK("vector", lambda e, bi=bi, PTx=PTx:
                               e.tensor_copy(ln1t[bi][:], PTx[:]),
                               deps=[t_tr, G_sp1[i - 2] if i >= 2 else None])
                sch.bank_read([bk], V_evt1[i])
                zbank = bi
                def zmm(e, bi=bi, zbank=zbank):
                    for c in range(KC):
                        e.matmul(PS[zbank][:, 0:48],
                                 ln1t[bi][:, c * 128:(c + 1) * 128],
                                 wz_sb[:, c * 48:(c + 1) * 48],
                                 start=(c == 0), stop=False)
                    return e.matmul(PS[zbank][:, 0:48], ones_r_bf[:], bzrow_bf[:],
                                    start=False, stop=True)
                T_z_l[i] = TK("tensor", zmm,
                              deps=[V_evt1[i], evWz[-1], t_bzbf]
                              + sch.bank_deps([zbank]))
                sch.bank_wrote([zbank])
                s_exp = TK("scalar", lambda e, bi=bi, zbank=zbank: e.activation(
                    zse[bi][:], PS[zbank][:, 0:32], AF.Exp, scale=-1.0),
                    deps=[T_z_l[i]])
                v_sig1 = TK("vector", lambda e, bi=bi: e.tensor_scalar(
                    zse[bi][:], zse[bi][:], 1.0, None, ALU.add, ALU.bypass),
                    deps=[s_exp])
                def _recip(e, bi=bi):
                    with nc.allow_low_precision(reason="bf16 sigmoid output"):
                        return e.reciprocal(zs[bi][:], zse[bi][:])
                s_sig = TK("vector", _recip, deps=[v_sig1])
                s_bx = TK("scalar", lambda e, bi=bi, zbank=zbank: e.activation(
                    zbx[bi][:], PS[zbank][:, 32:48], AF.Copy),
                    deps=[T_z_l[i], s_exp])
                v_zb = TK("vector", lambda e, bi=bi: e.tensor_tensor(
                    zbv[bi][:], zbx[bi][:], zs[bi][:, 16:32], ALU.mult),
                    deps=[s_bx, s_sig])
                sch.bank_read([zbank], s_exp)
                sch.bank_read([zbank], s_bx)
                PTo = PT2 if p == 0 else PT1
                bko = 7 - p
                t_zt = TK("tensor", lambda e, bi=bi, PTo=PTo: (
                    e.transpose(PTo[0:32, 0:128], zs[bi][:], idb[:]),
                    e.transpose(PTo[0:16, 128:256], zbv[bi][:], idb[:]))[-1],
                    deps=[s_sig, v_zb] + sch.bank_deps([bko]))
                sch.bank_wrote([bko])
                S_eA[i] = TK("scalar", lambda e, j=j, PTo=PTo: e.activation(
                    scanA[0:S, j * 128:(j + 1) * 128], PTo[0:16, 0:128],
                    AF.Copy),
                    deps=[t_zt, V_scan[b - 1] if b > 0 else None])
                V_eB[i] = TK("vector", lambda e, j=j, PTo=PTo: e.tensor_copy(
                    scanB[0:S, j * 128:(j + 1) * 128], PTo[0:16, 128:256]),
                    deps=[t_zt, S_eA[i], V_hsb[b - 1] if b > 0 else None])
                sch.bank_read([bko], S_eA[i])
                sch.bank_read([bko], V_eB[i])
                G_sp1[i] = TK("gpsimd", lambda e, i=i, bi=bi: e.dma_start(
                    out=d_ln1t[i], in_=ln1t[bi][:]), deps=[V_evt1[i]], dma=True)
                if j == TPB - 1:
                    V_scan[b] = TK("vector", lambda e: e.tensor_tensor_scan(
                        scanB[0:S, :], scanA[0:S, :], scanB[0:S, :], 0.0,
                        ALU.mult, ALU.add), deps=[S_eA[i], V_eB[i]])
                    V_hsb[b] = TK("vector", lambda e, b=b: e.tensor_copy(
                        hs_bf[0:S, b * T:(b + 1) * T], scanB[0:S, :]),
                        deps=[V_scan[b]])
            if it + 1 < NT:
                Y_x1l_[it + 1] = p1_load(it + 1)
        sch.phase = "p2"
        V_y2 = [None] * NT
        S_evxp = [None] * NT
        T_xp_l = [None] * NT
        T_xpT = [None] * NT
        T_y_l = [None] * NT
        T_y2_l = [None] * NT
        V_v = [None] * NT
        V_x1 = [None] * NT
        V_norm2 = [None] * NT
        V_evl2T = [None] * NT
        G_sp2a = [None] * NT
        G_sp2b = [None] * NT
        Yl1 = [None] * NT
        Yx2 = [None] * NT
        V_evxp = [None] * NT
        V_evxpT = [None] * NT
        S_tanh = [None] * NT
        V_evyT = [None] * NT
        T_yT_l = [None] * NT

        def p2_yl1(i):
            bi = i % 2
            return TK("sync", lambda e, i=i, bi=bi: e.dma_start(
                out=ln1t[bi][:], in_=d_ln1t[i]),
                deps=[G_sp1[i], T_xp_l[i - 2] if i >= 2 else T_z_l[NT - 1],
                      G_sp1[NT - 1] if i < 2 else None],
                dma=True)

        Yl1[0] = p2_yl1(0)
        for it in range(NT + 4):
            if it + 1 < NT:
                Yl1[it + 1] = p2_yl1(it + 1)
            if 0 <= it - 3 < NT:
                i = it - 3
                bi = i % 2
                x3 = i % 3
                V_y2[i] = TK("vector", lambda e, bi=bi, x3=x3: (
                    e.scalar_tensor_tensor(y2f[bi][:, 0:512], PS[4][:, 0:512], 1.0,
                                           xpf[x3][:, 0:512], ALU.mult, ALU.add),
                    e.scalar_tensor_tensor(y2f[bi][:, 512:1024], PS[5][:, 0:512],
                                           1.0, xpf[x3][:, 512:1024],
                                           ALU.mult, ALU.add))[-1],
                    deps=[T_y2_l[i], S_evxp[i]])
                sch.bank_read([4, 5], V_y2[i])
            if 0 <= it - 2 < NT:
                i = it - 2
                bi = i % 2
                Yx2[i] = TK("sync", lambda e, i=i, bi=bi: e.dma_start(
                    out=xb[bi][:], in_=d_x[i * 128:(i + 1) * 128, :]),
                    deps=[V_v[i - 2] if i >= 2 else V_norm1[NT - 1]], dma=True)
            if 0 <= it - 1 < NT:
                i = it - 1
                bi = i % 2
                T_xpT[i] = TK("tensor", lambda e, bi=bi: [
                    e.transpose(PT1[:, c * 128:(c + 1) * 128],
                                xpb[bi][:, c * 128:(c + 1) * 128], idb[:])
                    for c in range(KC)][-1],
                    deps=[V_evxp[i]] + sch.bank_deps([6]))
                sch.bank_wrote([6])
                V_evxpT[i] = TK("scalar", lambda e, bi=bi: e.activation(
                    xpt[bi][:], PT1[:], AF.Copy),
                    deps=[T_xpT[i], T_y_l[i - 2] if i >= 2 else None])
                sch.bank_read([6], V_evxpT[i])
            if 0 <= it - 2 < NT:
                i = it - 2
                bi = i % 2
                x3 = i % 3
                T_yT_l[i] = TK("tensor", lambda e, bi=bi: [
                    e.transpose(PT2[:, c * 128:(c + 1) * 128],
                                ybf[bi][:, c * 128:(c + 1) * 128], idb[:])
                    for c in range(KC)][-1],
                    deps=[S_tanh[i]] + sch.bank_deps([7]))
                sch.bank_wrote([7])
                V_evyT[i] = TK("scalar", lambda e, bi=bi: e.activation(
                    yt[bi][:], PT2[:], AF.Copy),
                    deps=[T_yT_l[i], T_y2_l[i - 2] if i >= 2 else None])
                sch.bank_read([7], V_evyT[i])
            if 0 <= it - 3 < NT:
                i = it - 3
                bi = i % 2
                V_v[i] = TK("vector", lambda e, bi=bi: e.tensor_tensor(
                    vf[bi][:], xb[bi][:], bet_bc[:], ALU.add),
                    deps=[Yx2[i], evgb[-1], G_sp2a[i - 2] if i >= 2 else None])
                rs2 = ln_stats(y2f[bi], bi, V_y2[i])
                nmr_t = TK("vector", lambda e, bi=bi: e.tensor_scalar(
                    nmr[bi][:], bnag[bi][:, 0:1], rstd[bi][:], -1.0,
                    ALU.mult, ALU.mult), deps=[rs2])
                v_t1 = TK("scalar", lambda e, bi=bi: e.activation(
                    y2f[bi][:], y2f[bi][:], AF.Identity,
                    bias=nmr[bi][:], scale=rstd[bi][:]),
                    deps=[nmr_t, rs2])
                v_u = TK("vector", lambda e, bi=bi: e.tensor_tensor(
                    y2f[bi][:], y2f[bi][:], gam_bc[:], ALU.mult), deps=[v_t1])
                V_x1[i] = TK("vector", lambda e, bi=bi: e.tensor_tensor(
                    vf[bi][:], y2f[bi][:], vf[bi][:], ALU.add),
                    deps=[v_u, V_v[i]])
                rs3 = ln_stats(vf[bi], bi, V_x1[i])
                nm2_t = TK("vector", lambda e, bi=bi: e.tensor_scalar(
                    nmr[bi][:], bnag[bi][:, 0:1], rstd[bi][:], -1.0,
                    ALU.mult, ALU.mult), deps=[rs3])
                V_norm2[i] = TK("scalar", lambda e, bi=bi: e.activation(
                    ln2bf[bi][:], vf[bi][:], AF.Identity,
                    bias=nmr[bi][:], scale=rstd[bi][:]),
                    deps=[nm2_t, rs3])
            if it < NT:
                i = it
                bi = i % 2
                x3 = i % 3
                def xpmm(e, bi=bi):
                    for half in range(2):
                        for c in range(KC):
                            e.matmul(PS[half][:, 0:512],
                                     ln1t[bi][:, c * 128:(c + 1) * 128],
                                     win_sb[:, c * H + half * 512:c * H + half * 512 + 512],
                                     start=(c == 0), stop=False)
                        last = e.matmul(PS[half][:, 0:512], ones_r_bf[:],
                                        brow_bf["bin"][0:1, half * 512:half * 512 + 512],
                                        start=False, stop=True)
                    return last
                T_xp_l[i] = TK("tensor", xpmm,
                               deps=[Yl1[i], L["binbf"]] + sch.bank_deps([0, 1]))
                sch.bank_wrote([0, 1])
                S_evxp[i] = TK("scalar", lambda e, x3=x3: (
                    e.activation(xpf[x3][:, 0:512], PS[0][:, 0:512], AF.Copy),
                    e.activation(xpf[x3][:, 512:1024], PS[1][:, 0:512], AF.Copy))[-1],
                    deps=[T_xp_l[i], V_y2[i - 3] if i >= 3 else None])
                V_evxp[i] = TK("scalar", lambda e, bi=bi, x3=x3: e.activation(
                    xpb[bi][:], xpf[x3][:], AF.Copy),
                    deps=[S_evxp[i], T_xpT[i - 2] if i >= 2 else None])
                sch.bank_read([0, 1], S_evxp[i])
            if 0 <= it - 4 < NT:
                i = it - 4
                bi = i % 2
                t_l2T = TK("tensor", lambda e, bi=bi: [
                    e.transpose(PT1[:, c * 128:(c + 1) * 128],
                                ln2bf[bi][:, c * 128:(c + 1) * 128], idb[:])
                    for c in range(KC)][-1],
                    deps=[V_norm2[i]] + sch.bank_deps([6]))
                sch.bank_wrote([6])
                V_evl2T[i] = TK("scalar", lambda e, bi=bi: e.activation(
                    ln2t[bi][:], PT1[:], AF.Copy),
                    deps=[t_l2T, G_sp2b[i - 2] if i >= 2 else None])
                sch.bank_read([6], V_evl2T[i])
                G_sp2a[i] = TK("gpsimd", lambda e, i=i, bi=bi: e.dma_start(
                    out=d_x1[i], in_=vf[bi][:]), deps=[V_norm2[i]], dma=True)
                G_sp2b[i] = TK("gpsimd", lambda e, i=i, bi=bi: e.dma_start(
                    out=d_ln2t[i], in_=ln2t[bi][:]), deps=[V_evl2T[i]], dma=True)

            if 0 <= it - 1 < NT:
                i = it - 1
                bi = i % 2
                b, j = divmod(i, TPB)
                def ymm(e, bi=bi, b=b, j=j):
                    for half in range(2):
                        for c in range(KC):
                            e.matmul(PS[2 + half][:, 0:512],
                                     xpt[bi][:, c * 128:(c + 1) * 128],
                                     wd_sb[:, c * H + half * 512:c * H + half * 512 + 512],
                                     start=(c == 0), stop=False)
                        e.matmul(PS[2 + half][:, 0:512],
                                 hs_bf[0:S, b * T + j * 128:b * T + (j + 1) * 128],
                                 cmat_sb[0:S, half * 512:half * 512 + 512],
                                 start=False, stop=False)
                        last = e.matmul(PS[2 + half][:, 0:512], ones_r_bf[:],
                                        brow_bf["bd"][0:1, half * 512:half * 512 + 512],
                                        start=False, stop=True)
                    return last
                T_y_l[i] = TK("tensor", ymm,
                              deps=[V_evxpT[i], V_hsb[b], L["bdbf"], L["cmat"]]
                              + sch.bank_deps([2, 3]))
                sch.bank_wrote([2, 3])
                S_tanh[i] = TK("scalar", lambda e, bi=bi: (
                    e.activation(ybf[bi][:, 0:512], PS[2][:, 0:512], AF.Tanh),
                    e.activation(ybf[bi][:, 512:1024], PS[3][:, 0:512], AF.Tanh))[-1],
                    deps=[T_y_l[i], T_yT_l[i - 2] if i >= 2 else None])
                sch.bank_read([2, 3], S_tanh[i])
            if 0 <= it - 2 < NT:
                i = it - 2
                bi = i % 2
                x3 = i % 3
                def y2mm(e, bi=bi):
                    for half in range(2):
                        for c in range(KC):
                            e.matmul(PS[4 + half][:, 0:512],
                                     yt[bi][:, c * 128:(c + 1) * 128],
                                     wout_sb[:, c * H + half * 512:c * H + half * 512 + 512],
                                     start=(c == 0), stop=False)
                        last = e.matmul(PS[4 + half][:, 0:512], ones_r_bf[:],
                                        brow_bf["bout"][0:1, half * 512:half * 512 + 512],
                                        start=False, stop=True)
                    return last
                T_y2_l[i] = TK("tensor", y2mm,
                               deps=[V_evyT[i], L["boutbf"]] + sch.bank_deps([4, 5]))
                sch.bank_wrote([4, 5])
        endP2 = sch.fence()
        sch.phase = "p3"

        es_p0.close()
        es_a.close()
        es_b = ExitStack()
        EB = es_b.enter_context
        bup_raw = EB(nc.sbuf_tensor("bup_raw", [NE, H], F32))
        sqacc = EB(nc.sbuf_tensor("sqacc", [128, NE * KC * T3PB], F32))
        lossm = EB(nc.sbuf_tensor("lossm", [128, 32], F32))
        lossm_bf = EB(nc.sbuf_tensor("lossm_bf", [128, 32], BF16))
        losse = EB(nc.sbuf_tensor("losse", [32, 128], F32))
        lossc = EB(nc.sbuf_tensor("lossc", [32, 1], F32))
        loss8 = EB(nc.sbuf_tensor("loss8", [1, NE], F32))
        minv = EB(nc.sbuf_tensor("minv", [1, 1], F32))
        oh8 = EB(nc.sbuf_tensor("oh8", [1, NE], F32))
        tmp8 = EB(nc.sbuf_tensor("tmp8", [1, NE], F32))
        tmp8b = EB(nc.sbuf_tensor("tmp8b", [1, NE], F32))
        adp8 = EB(nc.sbuf_tensor("adp8", [1, NE], F32))
        adp8_bf = EB(nc.sbuf_tensor("adp8_bf", [1, NE], BF16))
        c8col = EB(nc.sbuf_tensor("c8col", [NE, 1], F32))
        c_all = EB(nc.sbuf_tensor("c_all", [128, NE], F32))
        wdsel_a = [EB(nc.sbuf_tensor("wdsel0", [128, KC * R], F32))]
        wdsel_bf = [EB(nc.sbuf_tensor(f"wdsel_bf{j}", [128, KC * R], BF16))
                    for j in range(2)]
        wusel_a = [EB(nc.sbuf_tensor("wusel0", [64, H], F32))]
        wusel_bf = [EB(nc.sbuf_tensor(f"wusel_bf{j}", [64, H], BF16))
                    for j in range(2)]
        tmp8r = EB(nc.sbuf_tensor("tmp8r", [NE, R], BF16))
        tmp8h = EB(nc.sbuf_tensor("tmp8h", [NE, H], BF16))
        bsel_row = EB(nc.sbuf_tensor("bsel_row", [1, R + H], F32))
        bselH = EB(nc.sbuf_tensor("bselH", [1, H], F32))
        bsel_bf = [EB(nc.sbuf_tensor(f"bsel_bf{j}", [1, R + H], BF16))
                    for j in range(2)]
        t_lmz = TK("vector", lambda e: e.memset(lossm[:], 0.0), deps=endP2)
        wenc_sb = EB(nc.sbuf_tensor("wenc_sb", [128, NE * KC * HE], BF16))
        wdec_sb = EB(nc.sbuf_tensor("wdec_sb", [128, NE * 2 * H], BF16))
        wbase_sb = EB(nc.sbuf_tensor("wbase_sb", [128, KC * H], BF16))
        wdn_sb = EB(nc.sbuf_tensor("wdn_sb", [128, NE * KC * R], BF16))
        wup_sb = EB(nc.sbuf_tensor("wup_sb", [64, NE * H], BF16))
        l2t3 = [EB(nc.sbuf_tensor(f"l2t3_{j}", [128, 4 * H], BF16)) for j in range(2)]
        hta = [EB(nc.sbuf_tensor(f"hta{j}", [128, 512], BF16)) for j in range(2)]
        htb = [EB(nc.sbuf_tensor(f"htb{j}", [128, 512], BF16)) for j in range(2)]
        dsc = [EB(nc.sbuf_tensor(f"dsc{j}", [128, 512], F32)) for j in range(2)]
        ssc = [EB(nc.sbuf_tensor(f"ssc{j}", [128, 512], F32)) for j in range(2)]
        x1l = [EB(nc.sbuf_tensor(f"x1l{j}", [128, H], F32)) for j in range(3)]
        l2tf = [EB(nc.sbuf_tensor(f"l2tf{j}", [128, H], BF16)) for j in range(3)]
        midb = [EB(nc.sbuf_tensor(f"midb{j}", [128, R], BF16)) for j in range(2)]
        midt = [EB(nc.sbuf_tensor(f"midt{j}", [64, 128], BF16)) for j in range(2)]
        outb = [EB(nc.sbuf_tensor(f"outb{j}", [128, H], F32)) for j in range(2)]

        ffn_load_list = []
        for n in range(NE):
            ffn_load_list.append(
                (f"wenc{n}", wenc_sb[:, n * KC * HE:(n + 1) * KC * HE],
                 d_wenc[n].rearrange("(c p) h -> p c h", p=128)))
            ffn_load_list.append(
                (f"wdec{n}", wdec_sb[:, n * 2 * H:(n + 1) * 2 * H],
                 d_wdec[n].rearrange("(rc p) f -> p rc f", p=128)))
        for n in range(NE):
            ffn_load_list.append(
                (f"wdn{n}", wdn_sb[:, n * KC * R:(n + 1) * KC * R],
                 d_wdn[n].rearrange("(c p) r -> p c r", p=128)))
        ffn_load_list.append(
            ("wbase", wbase_sb[:], d_wbase.rearrange("(c p) f -> p c f", p=128)))
        ffn_load_list.append(
            ("wup", wup_sb[:], d_wup.rearrange("n r f -> r n f")))
        for nm, dst, src in ffn_load_list:
            L[nm] = TK("gpsimd", lambda e, dst=dst, src=src:
                       e.dma_start(out=dst, in_=src), deps=endP0 + endP2, dma=True)
        L["wenc"] = L[f"wenc{NE - 1}"]
        L["wdec"] = L[f"wdec{NE - 1}"]
        L["wdn"] = L[f"wdn{NE - 1}"]
        L["bup"] = TK("sync", lambda e: e.dma_start(out=bup_raw[:], in_=d_bup),
                      deps=endP2, dma=True)

        state = {
            "V_red_prev": None, "prev_sel": {}, "sel_out": {},
            "ht_last_reader": [None, None, None, None],
            "dsc_last_sq": [None, None],
            "tile3_last": {},
        }
        T_out_l = [None] * NT
        V_out_l = [None] * NT
        T_midT_l = [None] * NT
        S_relu2 = [None] * NT
        G_out = [None] * NT
        Yx1_l = [None] * NT
        Y_l2 = [None] * NT3
        S_sq_last_b = [None] * NB

        def emit_p3(b, inject=None):
            S_sq_last = {"scalar": None, "vector": None}

            def emit_dec(p):
                n, ni, kk, k = p["n"], p["ni"], p["kk"], p["k"]
                l2view = p["l2view"]
                last_vdiff = None
                last_dec = None

                def dec_w(e, n=n, ni=ni, d=0):
                    bank = 2 + (d % 4)
                    e.matmul(
                        PS[bank][:, 0:512],
                        wdec_sb[:, (n * 2) * H + d * 128:
                                (n * 2) * H + d * 128 + 128],
                        hta[ni][:], start=True, stop=False)
                    return e.matmul(
                        PS[bank][:, 0:512],
                        wdec_sb[:, (n * 2 + 1) * H + d * 128:
                                (n * 2 + 1) * H + d * 128 + 128],
                        htb[ni][:], start=False, stop=True)
                for d in range(KC):
                    bank = 2 + (d % 4)
                    bde = sch.bank_deps([bank])
                    t_w = TK("tensor", lambda e, f=dec_w, d=d: f(e, d=d),
                             deps=[p["s_reluA"], p["s_reluB"],
                                   L[f"wdec{n}"]] + bde)
                    sch.bank_wrote([bank])
                    di = d % 2
                    v_diff = TK("vector", lambda e, n=n, d=d, di=di,
                                bank=bank, l2view=l2view:
                                e.scalar_tensor_tensor(
                                    dsc[di][:], PS[bank][:, 0:512],
                                    bdecT[:, d * NE + n:d * NE + n + 1],
                                    l2view[:, :, d, :],
                                    ALU.add, ALU.subtract),
                                deps=[t_w, state["dsc_last_sq"][di],
                                      t_evbtr])
                    sch.bank_read([bank], v_diff)
                    col = n * KC * T3PB + kk * KC + d
                    red_dep = (state["V_red_prev"]
                               if (n == 0 and d == 0 and kk == 0)
                               else None)
                    if d % 4 != 3:
                        s_sq = TK("scalar", lambda e, di=di, col=col:
                                  e.activation(
                                      ssc[di][:], dsc[di][:], AF.Square,
                                      accum_out=sqacc[:, col:col + 1]),
                                  deps=[v_diff, red_dep])
                    else:
                        s_sq = TK("vector", lambda e, di=di, col=col:
                                  e.scalar_tensor_tensor(
                                      ssc[di][:], dsc[di][:], 1.0,
                                      dsc[di][:], ALU.mult, ALU.mult,
                                      accum_out=sqacc[:, col:col + 1]),
                                  deps=[v_diff, red_dep])
                    state["dsc_last_sq"][di] = s_sq
                    S_sq_last["scalar" if d % 4 != 3 else "vector"] = s_sq
                    last_vdiff = v_diff
                    last_dec = t_w
                state["ht_last_reader"][ni * 2] = last_dec
                state["ht_last_reader"][ni * 2 + 1] = last_dec
                if n == NE - 1:
                    state["tile3_last"][k] = [last_dec, last_vdiff]

            pend = None
            for kk in range(T3PB):
                if kk == 1 and inject is not None:
                    inject()
                    inject = None
                k = b * T3PB + kk
                ki = k % 2
                Y_l2[k] = TK("sync", lambda e, k=k, ki=ki: e.dma_start(
                    out=l2t3[ki][:],
                    in_=d_ln2t[4 * k:4 * k + 4].rearrange("t p f -> p t f")),
                    deps=[G_sp2b[4 * k + 3]]
                    + (state["tile3_last"].pop(k - 2, []) if k >= 2 else [])
                    + (endP2 if k == 0 else []), dma=True)
                l2view = l2t3[ki][:].rearrange("p (t c f) -> p t c f", t=4, c=KC)
                for n in range(NE):
                    ni = n % 2

                    def enc_fn(e, n=n, l2view=l2view, hh=0):
                        for c in range(KC):
                            last = e.matmul(
                                PS[hh][:, 0:512],
                                wenc_sb[:, (n * KC + c) * HE + hh * 128:
                                        (n * KC + c) * HE + hh * 128 + 128],
                                l2view[:, :, c, :],
                                start=(c == 0), stop=(c == KC - 1))
                        return last
                    t_encA = TK("tensor",
                                lambda e, f=enc_fn: f(e, hh=0),
                                deps=[Y_l2[k], L[f"wenc{n}"],
                                      state["ht_last_reader"][ni * 2]]
                                + sch.bank_deps([0]))
                    sch.bank_wrote([0])
                    s_reluA = TK("scalar", lambda e, n=n, ni=ni: e.activation(
                        hta[ni][:], PS[0][:, 0:512], AF.Relu,
                        bias=bencT[:, n:n + 1]),
                        deps=[t_encA, state["ht_last_reader"][ni * 2], t_evbtr])
                    sch.bank_read([0], s_reluA)
                    t_encB = TK("tensor",
                                lambda e, f=enc_fn: f(e, hh=1),
                                deps=[Y_l2[k], L[f"wenc{n}"],
                                      state["ht_last_reader"][ni * 2 + 1]]
                                + sch.bank_deps([1]))
                    sch.bank_wrote([1])
                    s_reluB = TK("scalar", lambda e, n=n, ni=ni: e.activation(
                        htb[ni][:], PS[1][:, 0:512], AF.Relu,
                        bias=bencT[:, NE + n:NE + n + 1]),
                        deps=[t_encB, state["ht_last_reader"][ni * 2 + 1], t_evbtr])
                    sch.bank_read([1], s_reluB)
                    newp = {"n": n, "ni": ni, "kk": kk, "k": k,
                            "l2view": l2view, "s_reluA": s_reluA,
                            "s_reluB": s_reluB}
                    if pend is not None:
                        emit_dec(pend)
                    pend = newp
            emit_dec(pend)
            if inject is not None:
                inject()
            S_sq_last_b[b] = S_sq_last

        def emit_sel_a(b):
            prev_sel = state["prev_sel"]
            V_red = TK("vector", lambda e: [
                e.tensor_reduce(lossm[:, n:n + 1],
                                sqacc[:, n * KC * T3PB:(n + 1) * KC * T3PB],
                                AX.X, ALU.add) for n in range(NE)][-1],
                deps=[S_sq_last_b[b]["scalar"], S_sq_last_b[b]["vector"],
                      t_lmz, prev_sel.get("t_ltr")])
            state["V_red_prev"] = V_red
            v_lmbf = TK("vector", lambda e: e.tensor_copy(lossm_bf[:], lossm[:]),
                        deps=[V_red, prev_sel.get("t_ltr")])
            t_ltr = TK("tensor", lambda e: e.transpose(PT2[0:32, 0:128],
                                                       lossm_bf[:], idb[:]),
                       deps=[v_lmbf] + sch.bank_deps([7]))
            sch.bank_wrote([7])
            state["sel_mid"] = {"prev": prev_sel, "t_ltr": t_ltr, "b": b}

        def emit_sel_rest(b):
            mid = state.pop("sel_mid")
            assert mid["b"] == b
            prev_sel = mid["prev"]
            t_ltr = mid["t_ltr"]
            s_elt = TK("scalar", lambda e: e.activation(losse[:],
                                                        PT2[0:32, 0:128], AF.Copy),
                       deps=[t_ltr, prev_sel.get("v_lred")])
            sch.bank_read([7], s_elt)
            v_lred = TK("vector", lambda e: e.tensor_reduce(lossc[:], losse[:],
                                                            AX.X, ALU.add),
                        deps=[s_elt, prev_sel.get("g_lflip")])
            g_lflip = TK("gpsimd", lambda e: e.dma_start(
                out=loss8[:], in_=lossc[0:NE, 0:1],
                allow_slow_non_contiguous=True),
                deps=[v_lred, prev_sel.get("v_oh"), prev_sel.get("v_adpbf")],
                dma=True)
            v_min = TK("vector", lambda e: e.tensor_reduce(minv[:], loss8[:],
                                                           AX.X, ALU.min),
                       deps=[g_lflip])
            v_oh = TK("vector", lambda e: e.tensor_scalar(
                oh8[:], loss8[:], minv[:], None, ALU.is_le, ALU.bypass),
                deps=[v_min])
            prev = v_oh
            for kkk in range(NE):
                t_eq = TK("vector", lambda e, kkk=kkk: e.tensor_scalar(
                    tmp8[:], cidx_f[:], float(kkk), None, ALU.is_equal, ALU.bypass),
                    deps=[prev, t_cidx,
                          prev_sel.get("g_cflip") if kkk == 0 else None])
                t_m = TK("vector", lambda e: e.tensor_tensor(
                    tmp8b[:], tmp8[:], oh8[:], ALU.mult), deps=[t_eq])
                prev = TK("vector", lambda e, kkk=kkk: e.tensor_reduce(
                    adp8[0:1, kkk:kkk + 1], tmp8b[:], AX.X, ALU.add), deps=[t_m])
            v_adpbf = TK("vector", lambda e: e.tensor_copy(adp8_bf[:], adp8[:]),
                         deps=[prev])
            g_cflip = TK("gpsimd", lambda e: e.dma_start(
                out=c8col[:], in_=adp8[0:1, 0:NE],
                allow_slow_non_contiguous=True),
                deps=[prev, prev_sel.get("v_tr"), prev_sel.get("v_th")], dma=True)
            state["sel_mid2"] = {"b": b, "prev": prev_sel, "t_ltr": t_ltr,
                                 "g_lflip": g_lflip, "v_oh": v_oh,
                                 "v_adpbf": v_adpbf, "g_cflip": g_cflip,
                                 "v_lred": v_lred}

        def emit_sel_rest2(b):
            mid = state.pop("sel_mid2")
            assert mid["b"] == b
            prev_sel = mid["prev"]
            t_ltr = mid["t_ltr"]
            g_lflip = mid["g_lflip"]
            v_oh = mid["v_oh"]
            v_adpbf = mid["v_adpbf"]
            g_cflip = mid["g_cflip"]
            v_lred = mid["v_lred"]
            t_cbc = TK("tensor", lambda e: e.matmul(
                PS[2][:, 128:128 + NE], ones_r_bf[:], adp8_bf[:],
                start=True, stop=True), deps=[v_adpbf] + sch.bank_deps([2]))
            sch.bank_wrote([2])
            s_ecbc = TK("scalar", lambda e: e.activation(
                c_all[:], PS[2][:, 128:128 + NE], AF.Copy), deps=[t_cbc])
            sch.bank_read([2], s_ecbc)
            prevw = TK("vector", lambda e: e.tensor_scalar(
                wdsel_a[0][:], wdn_sb[:, 0:KC * R], c_all[:, 0:1], None,
                ALU.mult, ALU.bypass),
                deps=[s_ecbc, L["wdn"], prev_sel.get("t_out_last")])
            for n in range(1, NE):
                prevw = TK("vector", lambda e, n=n: e.scalar_tensor_tensor(
                    wdsel_a[0][:], wdn_sb[:, n * KC * R:(n + 1) * KC * R],
                    c_all[:, n:n + 1], wdsel_a[0][:], ALU.mult, ALU.add),
                    deps=[prevw])
            v_wdbf = TK("vector", lambda e: e.tensor_copy(
                wdsel_bf[b % 2][:], wdsel_a[0][:]), deps=[prevw])
            prevu = TK("vector", lambda e: e.tensor_scalar(
                wusel_a[0][:], wup_sb[0:64, 0:H], c_all[0:64, 0:1], None,
                ALU.mult, ALU.bypass),
                deps=[s_ecbc, L["wup"], prev_sel.get("t_out_last")])
            for n in range(1, NE):
                prevu = TK("vector", lambda e, n=n: e.scalar_tensor_tensor(
                    wusel_a[0][:], wup_sb[0:64, n * H:(n + 1) * H],
                    c_all[0:64, n:n + 1], wusel_a[0][:], ALU.mult, ALU.add),
                    deps=[prevu])
            v_wubf = TK("vector", lambda e: e.tensor_copy(
                wusel_bf[b % 2][:], wusel_a[0][:]), deps=[prevu])
            v_tr = TK("vector", lambda e: e.tensor_scalar(
                tmp8r[:], bdn_raw[:], c8col[:], None, ALU.mult, ALU.bypass),
                deps=[g_cflip, L["bdn"], prev_sel.get("t_bh_last")])
            v_th = TK("vector", lambda e: e.tensor_scalar(
                tmp8h[:], bup_raw[:], c8col[:], None, ALU.mult, ALU.bypass),
                deps=[g_cflip, L["bup"], prev_sel.get("t_bh_last")])
            t_bselR = TK("tensor", lambda e: e.matmul(
                PS[2][0:8, 256:256 + R], ones88[:], tmp8r[:], start=True, stop=True),
                deps=[v_tr, L["ones88"]] + sch.bank_deps([2]))
            sch.bank_wrote([2])
            s_ebR = TK("scalar", lambda e: e.activation(
                bsel_row[0:1, 0:R], PS[2][0:1, 256:256 + R], AF.Copy),
                deps=[t_bselR])
            sch.bank_read([2], s_ebR)
            prevh = s_ebR
            t_bh = None
            for half in range(2):
                t_bh = TK("tensor", lambda e, half=half: e.matmul(
                    PS[2][0:8, 0:512], ones88[:],
                    tmp8h[0:NE, half * 512:half * 512 + 512], start=True, stop=True),
                    deps=[v_th, prevh] + sch.bank_deps([2]))
                sch.bank_wrote([2])
                prevh = TK("scalar", lambda e, half=half: e.activation(
                    bsel_row[0:1, R + half * 512:R + half * 512 + 512],
                    PS[2][0:1, 0:512], AF.Copy), deps=[t_bh])
                sch.bank_read([2], prevh)
            v_cmb = TK("vector", lambda e: e.tensor_tensor(
                bselH[:], bsel_row[0:1, R:R + H], brow["bbase"][:], ALU.add),
                deps=[prevh, L["bbase"]])
            v_bselbf = TK("vector", lambda e: (
                e.tensor_copy(bsel_bf[b % 2][0:1, 0:R], bsel_row[0:1, 0:R]),
                e.tensor_copy(bsel_bf[b % 2][0:1, R:R + H], bselH[:]))[-1],
                deps=[v_cmb])
            state["prev_sel"] = {
                "t_ltr": t_ltr, "g_lflip": g_lflip, "v_oh": v_oh,
                "v_adpbf": v_adpbf, "g_cflip": g_cflip, "v_tr": v_tr,
                "v_th": v_th, "t_bh_last": t_bh, "v_lred": v_lred}
            state["sel_out"][b] = (v_wdbf, v_wubf, v_bselbf)

        def emit_f2(b, inject=None, inject2=None):
            v_wdbf, v_wubf, v_bselbf = state["sel_out"][b]
            S_relu2_l = [None] * (TPB + 1)
            t_mid_l = [None] * (TPB + 1)
            for jj in range(TPB + 1):
                if jj == 1 and inject is not None:
                    inject()
                    inject = None
                if jj == 9 and inject2 is not None:
                    inject2()
                    inject2 = None
                if jj < TPB:
                    j = jj
                    i = b * TPB + j
                    bi = i % 2
                    ti = i % 3
                    yl2f = TK("sync", lambda e, i=i, ti=ti: e.dma_start(
                        out=l2tf[ti][:], in_=d_ln2t[i]),
                        deps=[G_sp2b[i], T_out_l[i - 3] if i >= 3 else None]
                        + (endP2 if i == 0 else []), dma=True)
                    yx1 = TK("sync", lambda e, i=i, ti=ti: e.dma_start(
                        out=x1l[ti][:], in_=d_x1[i]),
                        deps=[G_sp2a[i], V_out_l[i - 3] if i >= 3 else None]
                        + (endP2 if i == 0 else []), dma=True)
                    Yx1_l[i] = yx1
                    def midmm(e, ti=ti):
                        for c in range(KC):
                            e.matmul(PS[0][:, 0:R],
                                     l2tf[ti][:, c * 128:(c + 1) * 128],
                                     wdsel_bf[b % 2][:, c * R:(c + 1) * R],
                                     start=(c == 0), stop=False)
                        return e.matmul(PS[0][:, 0:R], ones_r_bf[:],
                                        bsel_bf[b % 2][0:1, 0:R],
                                        start=False, stop=True)
                    t_mid = TK("tensor", midmm,
                               deps=[yl2f, v_wdbf, v_bselbf] + sch.bank_deps([0]))
                    sch.bank_wrote([0])
                    t_mid_l[j] = t_mid
                    S_relu2[i] = TK("scalar", lambda e, bi=bi: e.activation(
                        midb[bi][:], PS[0][:, 0:R], AF.Relu),
                        deps=[t_mid, T_midT_l[i - 2] if i >= 2 else None])
                    sch.bank_read([0], S_relu2[i])
                if jj >= 1:
                    j = jj - 1
                    i = b * TPB + j
                    bi = i % 2
                    ti = i % 3
                    T_midT_l[i] = TK("tensor", lambda e, bi=bi: e.transpose(
                        PT1[0:64, 0:128], midb[bi][:], idb[:]),
                        deps=[S_relu2[i]] + sch.bank_deps([6]))
                    sch.bank_wrote([6])
                    v_emt = TK("vector", lambda e, bi=bi: e.tensor_copy(
                        midt[bi][:], PT1[0:64, 0:128]),
                        deps=[T_midT_l[i], T_out_l[i - 2] if i >= 2 else None])
                    sch.bank_read([6], v_emt)
                    def outmm(e, bi=bi, ti=ti):
                        for half in range(2):
                            for c in range(KC):
                                e.matmul(PS[2 + half][:, 0:512],
                                         l2tf[ti][:, c * 128:(c + 1) * 128],
                                         wbase_sb[:, c * H + half * 512:c * H + half * 512 + 512],
                                         start=(c == 0), stop=False)
                            e.matmul(PS[2 + half][:, 0:512], midt[bi][:],
                                     wusel_bf[b % 2][0:64,
                                                     half * 512:half * 512 + 512],
                                     start=False, stop=False)
                            last = e.matmul(PS[2 + half][:, 0:512], ones_r_bf[:],
                                            bsel_bf[b % 2][0:1, R + half * 512:
                                                           R + half * 512 + 512],
                                            start=False, stop=True)
                        return last
                    T_out_l[i] = TK("tensor", outmm,
                                    deps=[v_emt, v_wubf, v_bselbf, L["wbase"]]
                                    + sch.bank_deps([2, 3]))
                    sch.bank_wrote([2, 3])
                    V_out_l[i] = TK("vector", lambda e, bi=bi, ti=ti: (
                        e.scalar_tensor_tensor(outb[bi][:, 0:512], PS[2][:, 0:512],
                                               1.0, x1l[ti][:, 0:512],
                                               ALU.mult, ALU.add),
                        e.scalar_tensor_tensor(outb[bi][:, 512:1024],
                                               PS[3][:, 0:512], 1.0,
                                               x1l[ti][:, 512:1024],
                                               ALU.mult, ALU.add))[-1],
                        deps=[T_out_l[i], Yx1_l[i],
                              G_out[i - 2] if i >= 2 else None])
                    sch.bank_read([2, 3], V_out_l[i])
                    G_out[i] = TK("gpsimd", lambda e, i=i, bi=bi: e.dma_start(
                        out=d_out[i * 128:(i + 1) * 128, :], in_=outb[bi][:]),
                        deps=[V_out_l[i]], dma=True)
            if inject is not None:
                inject()
            if inject2 is not None:
                inject2()
            state["prev_sel"]["t_out_last"] = T_out_l[b * TPB + TPB - 1]

        emit_p3(0)
        emit_sel_a(0)
        for b in range(1, NB):
            emit_p3(b, inject=(lambda: (emit_sel_rest(0), emit_sel_rest2(0)))
                    if b == 1 else None)
            emit_sel_a(b)
            emit_f2(b - 1, inject=lambda bb=b: emit_sel_rest(bb),
                    inject2=lambda bb=b: emit_sel_rest2(bb))
        emit_f2(NB - 1)

        with nc.Block("main") as blk:
            sch.emit(blk)

        es_b.close()
    return nc


_NC_CACHE = {}


def kernel(**inputs):
    T, NBtot, NCORES = 2048, 32, 8
    NBC = NBtot // NCORES
    key = (T, NBC)
    if key not in _NC_CACHE:
        _NC_CACHE[key] = build_program(T=T, NB=NBC)
    nc = _NC_CACHE[key]

    g = lambda k: np.asarray(inputs[k])
    x = g("L0_attn_x").astype(np.float32, copy=False)
    common = {
        "win": g("L0_attn_W_in"), "wb": g("L0_attn_W_B"),
        "wd": g("L0_attn_W_D"), "wout": g("L0_attn_W_out"),
        "wbase": g("L0_ffn_W_base"),
        "bmat": g("L0_attn_B_mat"), "cmat": g("L0_attn_C_mat"),
        "wig": g("L0_attn_W_ig"), "wfg": g("L0_attn_W_fg"),
        "wenc": g("L0_ffn_W_enc"), "wdec": g("L0_ffn_W_dec"),
        "wdn": g("L0_ffn_W_down"), "wup": g("L0_ffn_W_up"),
        "bin": g("L0_attn_b_in").reshape(1, H),
        "bb": g("L0_attn_b_B").reshape(1, H),
        "bd": g("L0_attn_b_D").reshape(1, H),
        "bout": g("L0_attn_b_out").reshape(1, H),
        "big": g("L0_attn_b_ig").reshape(1, S),
        "bfg": g("L0_attn_b_fg").reshape(1, S),
        "gam": g("L0_attn_gamma").reshape(1, H),
        "bet": g("L0_attn_beta").reshape(1, H),
        "bbase": g("L0_ffn_b_base").reshape(1, H),
        "benc": g("L0_ffn_b_enc"), "bdec": g("L0_ffn_b_dec"),
        "bdn": g("L0_ffn_b_down"), "bup": g("L0_ffn_b_up"),
        "cidx": g("L0_ffn_connected_idx").reshape(1, NE).astype(np.int32),
        "idf": np.eye(128, dtype=np.float32),
        "idb": np.eye(128, dtype=ml_dtypes.bfloat16),
        "ones_r_bf": np.ones((1, 128), dtype=ml_dtypes.bfloat16),
        "ones_r_f": np.ones((1, 128), dtype=np.float32),
        "ones_cell_bf": np.ones((1, 1), dtype=ml_dtypes.bfloat16),
        "ones88": np.ones((8, 8), dtype=ml_dtypes.bfloat16),
    }
    in_maps = []
    for c in range(NCORES):
        m = dict(common)
        m["x"] = np.ascontiguousarray(x[NBC * c:NBC * (c + 1)].reshape(NBC * T, H))
        in_maps.append(m)
    res = run_bass_kernel_spmd(nc, in_maps, core_ids=list(range(NCORES)))
    out = np.concatenate(
        [res.results[c]["out"].reshape(NBC, T, H) for c in range(NCORES)], axis=0)
    return out.astype(np.float32)
